# version 15
# baseline (speedup 1.0000x reference)
import numpy as np
import ml_dtypes

import concourse.bass as bass
import concourse.mybir as mybir
from concourse.bass_utils import run_bass_kernel_spmd

F32 = mybir.dt.float32
BF16 = mybir.dt.bfloat16
F16 = mybir.dt.float16
AF = mybir.ActivationFunctionType
OP = mybir.AluOpType
AX = mybir.AxisListType

H = 2048
S_OWN = 1024
S_KV = 2048
DEBUG_SPILLS = False
C23 = float(np.float32(1.5 * 2.0 ** 23))
C52 = float(np.float32(1.5 * 2.0 ** 52))
INV_N_FF = float(np.float32(1.0 / np.sqrt(8192.0)))
NEGBIG = -30000.0
TWO_PI = 2.0 * np.pi
CW1 = float(np.float32(6.28125))
_c2 = np.float64(np.float32(TWO_PI - 6.28125))
CW2 = float(np.float32(_c2 - (_c2 % np.float64(2.0 ** -22))))
CW3 = float(np.float32(TWO_PI - CW1 - CW2))


class Sem:
    __slots__ = ("h", "n")

    def __init__(self, nc, name):
        self.h = nc.alloc_semaphore(name)
        self.n = 0


class B:
    def __init__(self, nc):
        self.nc = nc
        self.o = {k: [] for k in "tvsgy"}

    def w(self, eng, sem, val=None):
        v = sem.n if val is None else val
        if v > 0:
            self.o[eng].append(lambda e, _s=sem.h, _v=v: e.wait_ge(_s, _v))

    def add(self, eng, fn, inc=None, n=1):
        if inc is None:
            self.o[eng].append(fn)
        else:
            inc.n += n
            self.o[eng].append(lambda e, _f=fn, _s=inc.h, _n=n: _f(e).then_inc(_s, _n))

    def fence(self, sems):
        for eng in "tvsgy":
            for sm in sems:
                self.w(eng, sm)

    def dma(self, eng, out_ap, in_ap, inc):
        self.add(eng, lambda e, _o=out_ap, _i=in_ap: e.dma_start(_o, _i), inc=inc, n=16)

    def play(self, blockname):
        with self.nc.Block(blockname) as blk:
            @blk.tensor
            def _(e):
                for f in self.o["t"]:
                    f(e)

            @blk.vector
            def _(e):
                for f in self.o["v"]:
                    f(e)

            @blk.scalar
            def _(e):
                for f in self.o["s"]:
                    f(e)

            @blk.gpsimd
            def _(e):
                for f in self.o["g"]:
                    f(e)

            @blk.sync
            def _(e):
                for f in self.o["y"]:
                    f(e)


def build_nc():
    nc = bass.Bass("TRN2", target_bir_lowering=False, detect_race_conditions=False)

    d_xT = nc.dram_tensor("xT", [H, S_KV], F32, kind="ExternalInput")
    d_xown = nc.dram_tensor("xown", [S_OWN, H], F32, kind="ExternalInput")
    d_qkw = nc.dram_tensor("qkw", [32, 128, 2048], F32, kind="ExternalInput")
    d_vw = nc.dram_tensor("vw", [4, 128, 8192], F32, kind="ExternalInput")
    d_wow = nc.dram_tensor("wow", [4, 128, 8192], F32, kind="ExternalInput")
    d_gw = nc.dram_tensor("gw", [64, 128, 2048], F32, kind="ExternalInput")
    d_uw = nc.dram_tensor("uw", [64, 128, 2048], F32, kind="ExternalInput")
    d_dw = nc.dram_tensor("dw", [4, 16, 128, 2048], F32, kind="ExternalInput")
    d_nw = nc.dram_tensor("nw", [128, 64], F32, kind="ExternalInput")
    d_posb = nc.dram_tensor("posb", [128, S_KV], F32, kind="ExternalInput")
    d_ksin = nc.dram_tensor("ksin", [128, S_KV], F32, kind="ExternalInput")
    d_kcos = nc.dram_tensor("kcos", [128, S_KV], F32, kind="ExternalInput")
    d_maskL = nc.dram_tensor("maskL", [128, 8 * 9 * 128], BF16, kind="ExternalInput")
    d_ident = nc.dram_tensor("identb", [128, 128], BF16, kind="ExternalInput")
    d_identf = nc.dram_tensor("identf", [128, 128], F32, kind="ExternalInput")
    d_h128 = nc.dram_tensor("h128", [128, 128], BF16, kind="ExternalInput")
    d_h128f = nc.dram_tensor("h128f", [128, 128], F32, kind="ExternalInput")
    d_misc = nc.dram_tensor("miscc", [128, 2], F32, kind="ExternalInput")
    d_out = nc.dram_tensor("out", [S_OWN, H], F32, kind="ExternalOutput")

    _spk = "ExternalOutput" if DEBUG_SPILLS else "Internal"
    d_qf = nc.dram_tensor("qf_sp", [16, 128, S_OWN], BF16, kind=_spk)
    d_kf = nc.dram_tensor("kf_sp", [16, 128, S_KV], BF16, kind=_spk)
    d_vsp = nc.dram_tensor("v_sp", [4, 16, 128, 512], BF16, kind=_spk)
    d_x1 = nc.dram_tensor("x1_sp", [S_OWN, H], F32, kind=_spk)
    d_int = nc.dram_tensor("int_sp", [64, 128, S_OWN], F32, kind=_spk)
    d_dwq = nc.dram_tensor("dwq_sp", [64, 128, 2048], BF16, kind="Internal")
    d_y = nc.dram_tensor("y_sp", [S_OWN, H], F32, kind=_spk)
    d_att = nc.dram_tensor("att_dbg", [16, 128, S_OWN], BF16, kind=_spk)
    d_kro = nc.dram_tensor("kro_dbg", [2, 128, S_KV], BF16, kind=_spk)
    d_qro = nc.dram_tensor("qro_dbg", [2, 128, S_OWN], BF16, kind=_spk)
    d_pt = nc.dram_tensor("pt_dbg", [2, 128, 9 * 128], BF16, kind=_spk)
    d_zq = nc.dram_tensor("zq_dbg", [64, 128, S_OWN], BF16, kind=_spk)
    d_st = nc.dram_tensor("st_dbg", [128, 704], F32, kind=_spk)

    SB_BASE = 17408
    def sb(name, shape, dtype, off):
        per = int(np.prod(shape[1:])) * mybir.dt.size(dtype)
        assert off + per <= 211900, (name, off, per)
        return nc.alloc_sbuf_tensor_at(name, shape, dtype, offset=SB_BASE + off)

    ones_b = sb("ones_b", [128, 1], BF16, 0)
    ones_rb = sb("ones_rb", [1, 128], BF16, 32)
    ones_f = sb("ones_f", [128, 1], F32, 320)
    ones_rf = sb("ones_rf", [1, 128], F32, 352)
    ones_h = sb("ones_h", [128, 1], F16, 896)
    identb = sb("identb_s", [128, 128], BF16, 928)
    identf = sb("identf_s", [128, 128], F32, 1184)
    h128c = sb("h128_s", [128, 128], BF16, 1696)
    h128f = sb("h128f_s", [128, 128], F32, 15360)
    nw = sb("nw_s", [128, 64], F32, 1952)
    misc = sb("misc_s", [128, 2], F32, 2208)
    cst = sb("cst", [128, 16], F32, 2240)
    vrow = sb("vrow", [1, 2048], F32, 2304)
    vrow2 = sb("vrow2", [1, 2048], F32, 10496)
    stA = sb("stA", [128, 64], F32, 18688)
    stB = sb("stB", [128, 64], F32, 18944)
    aws = sb("aws", [128, 12], F32, 19200)
    cosf = sb("cosf", [128, S_KV], F32, 20000)
    sinf = sb("sinf", [128, S_KV], F32, 28192)
    cosrs_k = sb("cosrs_k", [128, S_KV], BF16, 36384)
    sinrs_k = sb("sinrs_k", [128, S_KV], BF16, 40480)
    cosrs_q = sb("cosrs_q", [128, S_OWN], BF16, 44576)
    sinrs_q = sb("sinrs_q", [128, S_OWN], BF16, 46624)
    rsbk = sb("rsbk", [128, S_KV], F32, 48672)
    rsbq = sb("rsbq", [128, S_OWN], F32, 56864)

    A = 80000

    stg = sb("stg", [128, 5120], F32, A)

    kevb = [sb(f"kevb{i}", [128, S_KV], BF16, A + i * 4096) for i in range(2)]
    qevb = [sb(f"qevb{i}", [128, 512], BF16, A + 8192 + i * 1024) for i in range(4)]
    q2f = [sb(f"q2f{i}", [128, 512], F16, A + 12288 + i * 1024) for i in range(2)]
    qkwb = [sb(f"qkwb{i}", [128, 2048], BF16, A + 16384 + i * 4096) for i in range(2)]
    vwb = [sb(f"vwb{i}", [128, 8192], BF16, A + 24576 + i * 16384) for i in range(2)]
    xTt = [sb(f"xTt{k}", [128, S_KV], BF16, A + 57344 + k * 4096) for k in range(16)]

    wredb3 = [sb(f"wredb3{i}", [128, 1024], F32, A + 98304 + i * 4096) for i in range(8)]

    kraw = [sb(f"kraw{i}", [128, S_KV], BF16, A + i * 4096) for i in range(2)]
    qraw = [sb(f"qraw{i}", [128, S_OWN], BF16, A + 8192 + i * 2048) for i in range(2)]
    vsb = [sb(f"vsb{i}", [128, S_KV], BF16, A + 12288 + i * 4096) for i in range(2)]
    kswap = sb("kswap", [128, S_KV], BF16, A + 20480)
    qswap = sb("qswap", [128, S_OWN], BF16, A + 24576)
    m1k = sb("m1k", [128, S_KV], BF16, A + 26624)
    m2k = sb("m2k", [128, S_KV], BF16, A + 30720)
    krot = [sb(f"krot{i}", [128, S_KV], BF16, A + 34816 + i * 4096) for i in range(2)]
    m1q = sb("m1q", [128, S_OWN], BF16, A + 43008)
    m2q = sb("m2q", [128, S_OWN], BF16, A + 45056)
    qrot = [sb(f"qrot{i}", [128, S_OWN], BF16, A + 47104 + i * 2048) for i in range(2)]
    ptil = [sb(f"ptil{i}", [128, 9 * 128], BF16, 20000 + i * 2304) for i in range(4)]
    ptT = [sb(f"ptT{i}", [128, 9 * 128], BF16, A + 51200 + i * 2304) for i in range(2)]
    nms = sb("nms", [128, 32], F32, A + 60416)
    wcols = sb("wcols", [128, 384], F32, A + 60544)
    maskL = sb("maskLs", [128, 8 * 9 * 128], BF16, 61184)
    attn = [sb(f"attn{h}", [128, S_OWN], BF16, A + 65536 + h * 2048) for h in range(16)]

    y1t = [sb(f"y1t{i}", [128, H], F32, A + i * 8192) for i in range(8)]
    wowb = [sb(f"wowb{i}", [128, 8192], BF16, 20000 + i * 16384) for i in range(2)]
    xownb = [sb(f"xownb{i}", [128, H], F32, 52768 + i * 8192) for i in range(2)]
    sqscr = sb("sqscr", [128, H], F32, 69152)
    sqscr2 = sb("sqscr2", [128, H], F32, 20000)
    xqs = [sb(f"xqs{i}", [128, H], BF16, 28192 + i * 4096) for i in range(2)]
    arow = sb("arow", [1, 1024], F32, 36384)
    xqT = [sb(f"xqT{k}", [128, S_OWN], BF16, A + 65536 + k * 2048) for k in range(16)]
    cgb = sb("cgb", [128, S_OWN], F32, A + 98304)
    cub = sb("cub", [128, S_OWN], F32, A + 102400)
    amcb = sb("amcb", [128, S_OWN], F32, A + 106496)

    gwf = [sb(f"gwf{i}", [128, 2048], F32, 20000 + i * 8192) for i in range(2)]
    uwf = [sb(f"uwf{i}", [128, 2048], F32, 36384 + i * 8192) for i in range(2)]
    tsgg1 = [sb(f"tsgg1_{i}", [128, 2048], BF16, 52768 + i * 4096) for i in range(2)]
    tsgg2 = [sb(f"tsgg2_{i}", [128, 2048], BF16, A + 20480 + i * 4096) for i in range(2)]
    tsgu1 = [sb(f"tsgu1_{i}", [128, 2048], BF16, A + 28672 + i * 4096) for i in range(2)]
    tsgu2 = [sb(f"tsgu2_{i}", [128, 2048], BF16, A + 36864 + i * 4096) for i in range(2)]
    wqg = [sb(f"wqg{i}", [128, 2048], BF16, 60960 + i * 4096) for i in range(2)]
    wqu = [sb(f"wqu{i}", [128, 2048], BF16, 69152 + i * 4096) for i in range(2)]
    tgv = sb("tgv", [128, S_OWN], F32, A)
    tgs = sb("tgs", [128, S_OWN], F32, A + 4096)
    tuv = sb("tuv", [128, S_OWN], F32, A + 8192)
    intev = [sb(f"intev{i}", [128, S_OWN], F32, A + 12288 + i * 4096) for i in range(2)]

    zw = [sb(f"zw{k}", [128, 512], F32, 20000 + k * 2048) for k in range(64)]
    Q6 = 152000
    bscr = [sb(f"bscr{i}", [128, 512], F32, Q6 + i * 2048) for i in range(4)]
    mxz = sb("mxz", [128, 512], F32, Q6 + 8192)
    mnz = sb("mnz", [128, 512], F32, Q6 + 10240)
    absz = sb("absz", [128, 512], F32, Q6 + 12288)
    sbcz = sb("sbcz", [128, 512], F32, Q6 + 14336)
    zsc = [sb(f"zsc{i}", [128, 512], F32, Q6 + 16384 + i * 2048) for i in range(3)]
    zsc2 = [sb(f"zsc2_{i}", [128, 512], F32, Q6 + 22528 + i * 2048) for i in range(2)]
    zqe = [sb(f"zqe{i}", [128, 512], BF16, Q6 + 26624 + i * 1024) for i in range(4)]
    arow2 = sb("arow2", [1, 1024], F32, Q6 + 30720)
    dwf6h = [sb(f"dwf6h{i}", [128, 1024], F32, Q6 + 34816 + i * 4096) for i in range(2)]
    td61 = sb("td61", [128, 2048], BF16, Q6 + 43008)
    td62 = sb("td62", [128, 2048], BF16, Q6 + 47104)
    dwq6 = [sb(f"dwq6_{i}", [128, 2048], BF16, Q6 + 51200 + i * 4096) for i in range(2)]

    Q7 = 152000
    zq7 = [sb(f"zq7_{k}", [128, 512], BF16, 20000 + k * 1024) for k in range(64)]
    dwq4 = [sb(f"dwq4_{i}", [128, 2048], BF16, 85536 + i * 4096) for i in range(4)]
    yev = [sb(f"yev{i}", [128, 512], F32, Q7 + 32768 + i * 2048) for i in range(2)]
    x1rd = [sb(f"x1rd{i}", [128, 512], F32, Q7 + 36864 + i * 2048) for i in range(2)]
    sqs7 = sb("sqs7", [128, 512], F32, Q7 + 40960)
    yfin = [sb(f"yfin{i}", [128, H], F32, Q7 + i * 8192) for i in range(2)]
    ofin = [sb(f"ofin{i}", [128, H], F32, Q7 + 16384 + i * 8192) for i in range(2)]

    ps3 = [nc.alloc_psum_tensor(f"ps3_{i}", [128, 1536], F32) for i in range(2)]
    pp = [ps3[i // 3][:, (i % 3) * 512:(i % 3 + 1) * 512] for i in range(6)]
    pbf = [nc.alloc_psum_tensor(f"pbf{i}", [128, 1024], BF16) for i in range(2)]

    ld = Sem(nc, "ld")
    ldy = Sem(nc, "ldy")
    st = Sem(nc, "st")
    t1 = Sem(nc, "t1")
    t2 = Sem(nc, "t2")
    v1 = Sem(nc, "v1")
    v2 = Sem(nc, "v2")
    s1 = Sem(nc, "s1")
    s2 = Sem(nc, "s2")
    g1 = Sem(nc, "g1")
    ALLSEMS = [ld, ldy, st, t1, t2, v1, v2, s1, s2, g1]

    b = B(nc)
    b.dma("g", identb[:], d_ident[:], ld)
    b.dma("g", identf[:], d_identf[:], ld)
    b.dma("g", h128c[:], d_h128[:], ld)
    b.dma("g", h128f[:], d_h128f[:], ld)
    b.dma("g", nw[:], d_nw[:], ld)
    b.dma("g", misc[:], d_misc[:], ld)

    def _ms(e):
        e.memset(ones_b[:], 1.0)
        e.memset(ones_rb[:], 1.0)
        e.memset(ones_f[:], 1.0)
        e.memset(ones_rf[:], 1.0)
        e.memset(ones_h[:], 1.0)
        e.memset(cst[:, 0:1], C52)
        e.memset(cst[:, 1:2], 1.0)
        e.memset(cst[:, 6:7], -0.5)
        e.memset(cst[:, 8:9], -1.0)
        e.memset(cst[:, 9:10], 10000.0)
        return e.memset(cst[:, 14:15], 0.5)
    b.add("g", _ms, inc=g1)

    b.w("v", g1)
    b.add("v", lambda e: e.tensor_scalar(cst[:, 2:3], cst[:, 1:2], cst[:, 0:1], None, op0=OP.add), inc=v1)
    b.w("v", v1)
    b.add("v", lambda e: e.tensor_scalar(cst[:, 3:4], cst[:, 2:3], cst[:, 0:1], None, op0=OP.subtract), inc=v1)
    b.w("v", v1)
    b.add("v", lambda e: e.tensor_scalar(cst[:, 15:16], cst[:, 3:4], C52, None, op0=OP.mult), inc=v1)
    b.w("v", v1)
    b.add("v", lambda e: e.tensor_scalar(cst[:, 4:5], cst[:, 15:16], C23, None, op0=OP.add), inc=v1)
    b.w("v", v1)
    b.add("v", lambda e: e.tensor_scalar(cst[:, 5:6], cst[:, 4:5], -1.0, None, op0=OP.mult), inc=v1)
    b.w("v", v1)
    b.add("v", lambda e: e.tensor_scalar(cst[:, 3:4], cst[:, 4:5], 127.0, None, op0=OP.add), inc=v1)
    b.add("v", lambda e: e.tensor_scalar(cst[:, 2:3], cst[:, 1:2], 127.0, None, op0=OP.mult), inc=v1)

    b.w("s", g1)
    b.add("s", lambda e: e.activation(cst[:, 10:11], cst[:, 9:10], AF.Ln), inc=s1)
    b.w("v", s1)
    b.w("v", ld)
    b.add("v", lambda e: e.tensor_scalar(cst[:, 15:16], misc[:, 0:1], cst[:, 10:11], None, op0=OP.mult), inc=v1)
    b.w("s", v1)
    b.add("s", lambda e: e.activation(cst[:, 12:13], cst[:, 15:16], AF.Exp, scale=cst[:, 8:9]), inc=s1)
    b.w("v", s1)
    b.add("v", lambda e: e.tensor_scalar(cst[:, 13:14], cst[:, 12:13], misc[:, 1:2], None, op0=OP.mult), inc=v1)
    INVFS = v1.n

    for half in range(2):
        lo = half * 1024
        b.w("y", s1, s1.n)
        b.dma("y", stg[:, 0:1024], d_posb[:, lo:lo + 1024], ldy)
        b.dma("y", stg[:, 1024:2048], d_ksin[:, lo:lo + 1024], ldy)
        b.dma("y", stg[:, 2048:3072], d_kcos[:, lo:lo + 1024], ldy)
        b.w("v", ldy)
        b.w("v", v1, INVFS)
        b.add("v", lambda e: e.tensor_scalar(stg[:, 3072:4096], stg[:, 0:1024], cst[:, 13:14], None, op0=OP.mult), inc=v1)
        b.w("v", v1)
        b.add("v", lambda e: e.tensor_copy(stg[:, 4096:5120], stg[:, 3072:4096]), inc=v1)
        b.w("v", v1)
        b.add("v", lambda e: e.scalar_tensor_tensor(stg[:, 3072:4096], stg[:, 1024:2048], -CW1, stg[:, 3072:4096], op0=OP.mult, op1=OP.add))
        b.add("v", lambda e: e.scalar_tensor_tensor(stg[:, 3072:4096], stg[:, 1024:2048], -CW2, stg[:, 3072:4096], op0=OP.mult, op1=OP.add))
        b.add("v", lambda e: e.scalar_tensor_tensor(stg[:, 3072:4096], stg[:, 1024:2048], -CW3, stg[:, 3072:4096], op0=OP.mult, op1=OP.add))
        b.add("v", lambda e: e.scalar_tensor_tensor(stg[:, 4096:5120], stg[:, 2048:3072], -CW1, stg[:, 4096:5120], op0=OP.mult, op1=OP.add))
        b.add("v", lambda e: e.scalar_tensor_tensor(stg[:, 4096:5120], stg[:, 2048:3072], -CW2, stg[:, 4096:5120], op0=OP.mult, op1=OP.add))
        b.add("v", lambda e: e.scalar_tensor_tensor(stg[:, 4096:5120], stg[:, 2048:3072], -CW3, stg[:, 4096:5120], op0=OP.mult, op1=OP.add), inc=v1)
        b.w("s", v1)
        b.add("s", lambda e, _l=lo: e.activation(sinf[:, _l:_l + 1024], stg[:, 3072:4096], AF.Sin), inc=s1)
        b.add("s", lambda e, _l=lo: e.activation(cosf[:, _l:_l + 1024], stg[:, 4096:5120], AF.Sin), inc=s1)
    b.play("p0")

    b = B(nc)
    b.fence(ALLSEMS)
    for k in range(16):
        b.dma("g", xTt[k][:], d_xT[k * 128:(k + 1) * 128, :], ld)
    XT_LD = ld.n

    slab_t = {}
    evict_s = {}
    spill_y = {}

    for fc in range(4):
        buf = vwb[fc % 2]
        if fc >= 2:
            b.w("g", t1, slab_t[("v", fc - 2)])
        b.dma("g", buf[:], d_vw[fc], ld)
        ldn = ld.n
        b.w("t", ld, ldn)
        if fc == 0:
            b.w("t", ld, XT_LD)
        for tokt in range(16):
            ppb = pp[tokt % 2]
            keyp = ("v", fc, tokt - 2) if tokt >= 2 else ("v", fc - 1, 14 + tokt)
            if keyp in evict_s:
                b.w("t", s1, evict_s[keyp])

            def mk(_b=buf, _p=ppb, _tt=tokt):
                def fn(e):
                    ins = None
                    for kt in range(16):
                        ins = e.matmul(_p[:], xTt[kt][:, _tt * 128:(_tt + 1) * 128],
                                       _b[:, kt * 512:(kt + 1) * 512],
                                       start=(kt == 0), stop=(kt == 15))
                    return ins
                return fn
            b.add("t", mk(), inc=t1)
            b.w("s", t1)
            ev = qevb[tokt % 4]
            kspill = ("vs", fc, tokt - 4) if tokt >= 4 else ("vs", fc - 1, 12 + tokt)
            if kspill in spill_y:
                b.w("s", st, spill_y[kspill])
            b.add("s", lambda e, _p=ppb, _e=ev: e.copy(_e[:], _p[:]), inc=s1)
            evict_s[("v", fc, tokt)] = s1.n
            b.w("y", s1)
            b.dma("y", d_vsp[fc, tokt], ev[:], st)
            spill_y[("vs", fc, tokt)] = st.n
        slab_t[("v", fc)] = t1.n

    for sec in ("k", "q"):
        ftlist = list(range(16, 32)) if sec == "k" else list(range(0, 16))
        chunks = [0, 1, 2, 3] if sec == "k" else [2, 3]
        varps = [pp[2], pp[3], pp[4], pp[5]] if sec == "k" else [pp[2], pp[3]]
        dspill = d_kf if sec == "k" else d_qf
        if sec == "q":
            b.w("s", st, st.n)
            b.w("g", t1, t1.n)
        chev = {}
        for i, ft in enumerate(ftlist):
            buf = qkwb[i % 2]
            if i >= 2:
                b.w("g", t1, slab_t[(sec, i - 2)])
            b.dma("g", buf[:], d_qkw[ft], ld)
            ldn = ld.n
            kev = kevb[i % 2]
            if i >= 2:
                b.w("s", st, spill_y[(sec, i - 2)])
            for ci, ch in enumerate(chunks):
                gidx = i * len(chunks) + ci
                ppb = pp[gidx % 2]
                b.w("t", ld, ldn)
                if gidx >= 2:
                    b.w("t", s1, chev[gidx - 2])
                if sec == "k" and i == 0 and ci == 0:
                    b.w("t", ld, XT_LD)

                def mk(_b=buf, _p=ppb, _ch=ch):
                    def fn(e):
                        ins = None
                        for kt in range(16):
                            ins = e.matmul(_p[:], _b[:, kt * 128:(kt + 1) * 128],
                                           xTt[kt][:, _ch * 512:(_ch + 1) * 512],
                                           start=(kt == 0), stop=(kt == 15))
                        return ins
                    return fn
                b.add("t", mk(), inc=t1)
                b.w("s", t1)
                b.add("s", lambda e, _p=ppb, _e=kev, _c=ci: e.copy(_e[:, _c * 512:(_c + 1) * 512], _p[:]), inc=s1)
                q2 = q2f[(i * 4 + ci) % 2]
                b.add("s", lambda e, _p=ppb, _q=q2: e.activation(_q[:], _p[:], AF.Square), inc=s1)
                sqn = s1.n
                chev[gidx] = s1.n
                vp = varps[ci]
                b.w("t", s1, sqn)
                b.add("t", lambda e, _q=q2, _vp=vp, _st=(i == 0), _sp=(i == 15): e.matmul(
                    _vp[0:1, :], ones_h[:], _q[:], start=_st, stop=_sp), inc=t1)
            evict_s[(sec, i)] = s1.n
            b.w("y", s1)
            ncols = 2048 if sec == "k" else 1024
            b.dma("y", dspill[i], kev[:, 0:ncols], st)
            spill_y[(sec, i)] = st.n
            slab_t[(sec, i)] = t1.n
        b.w("s", t1)
        dstrow = vrow if sec == "k" else vrow2
        for ci in range(len(chunks)):
            b.add("s", lambda e, _c=ci, _d=dstrow, _vp=varps[ci]: e.copy(
                _d[0:1, _c * 512:(_c + 1) * 512], _vp[0:1, :]), inc=s1)

    b.add("v", lambda e: e.memset(cst[:, 7:8], 1e-6))
    b.add("v", lambda e: e.memset(cst[:, 15:16], float(1.0 / 2048.0)))
    b.add("v", lambda e: e.memset(cst[:, 11:12], float(128.0 * 1e-6)))
    b.add("v", lambda e: e.memset(cst[:, 12:13], float(1.0 / 16.0)), inc=v1)
    b.w("s", v1)
    b.w("s", s1, s1.n)
    b.add("s", lambda e: e.activation(vrow[0:1, :], vrow[0:1, :], AF.Sqrt,
                                      bias=cst[0:1, 7:8], scale=cst[0:1, 15:16]), inc=s1)
    b.add("s", lambda e: e.activation(vrow2[0:1, 0:1024], vrow2[0:1, 0:1024], AF.Sqrt,
                                      bias=cst[0:1, 11:12], scale=cst[0:1, 12:13]), inc=s1)
    b.w("v", s1)
    b.add("v", lambda e: e.reciprocal(vrow[0:1, :], vrow[0:1, :]), inc=v1)
    b.add("v", lambda e: e.reciprocal(vrow2[0:1, 0:1024], vrow2[0:1, 0:1024]), inc=v1)
    b.w("t", v1, v1.n)
    for ci in range(4):
        b.add("t", lambda e, _c=ci: e.matmul(pp[0][:, :], ones_rf[:], vrow[0:1, _c * 512:(_c + 1) * 512],
                                             start=True, stop=True), inc=t1)
        b.w("s", t1)
        b.add("s", lambda e, _c=ci: e.copy(rsbk[:, _c * 512:(_c + 1) * 512], pp[0][:, :]), inc=s1)
        b.w("t", s1)
    for ci in range(2):
        b.add("t", lambda e, _c=ci: e.matmul(pp[1][:, :], ones_rf[:], vrow2[0:1, _c * 512:(_c + 1) * 512],
                                             start=True, stop=True), inc=t1)
        b.w("s", t1)
        b.add("s", lambda e, _c=ci: e.copy(rsbq[:, _c * 512:(_c + 1) * 512], pp[1][:, :]), inc=s1)
        b.w("t", s1)
    b.w("v", s1)
    b.add("v", lambda e: e.tensor_mul(cosrs_k[:], cosf[:], rsbk[:]), inc=v1)
    b.add("v", lambda e: e.tensor_mul(sinrs_k[:], sinf[:], rsbk[:]), inc=v1)
    b.add("v", lambda e: e.tensor_mul(cosrs_q[:], cosf[:, 1024:2048], rsbq[:]), inc=v1)
    b.add("v", lambda e: e.tensor_mul(sinrs_q[:], sinf[:, 1024:2048], rsbq[:]), inc=v1)
    b.play("p2")

    b = B(nc)
    b.fence(ALLSEMS)
    b.dma("g", maskL[:], d_maskL[:], ld)
    MASK_LD = ld.n

    amsrc = ([d_gw[ft][:, hf * 1024:(hf + 1) * 1024] for ft in range(64) for hf in range(2)]
             + [d_uw[ft][:, hf * 1024:(hf + 1) * 1024] for ft in range(64) for hf in range(2)]
             + [d_dw[fo, kc][:, hf * 1024:(hf + 1) * 1024]
                for fo in range(4) for kc in range(16) for hf in range(2)])
    am_cons = {}
    am_ldn = {}

    def am_issue(c):
        buf = wredb3[c % 8]
        if c - 8 in am_cons:
            b.w("g", *am_cons[c - 8])
        b.dma("g", buf[:], amsrc[c], ld)
        am_ldn[c] = ld.n

    def am_consume(c):
        buf = wredb3[c % 8]
        if c % 2 == 0:
            b.w("s", ld, am_ldn[c])
            b.add("s", lambda e, _b2=buf, _co=c: e.activation(
                _b2[:], _b2[:], AF.Abs, accum_out=wcols[:, _co:_co + 1]), inc=s2)
            am_cons[c] = (s2, s2.n)
        else:
            b.w("v", ld, am_ldn[c])
            b.add("v", lambda e, _b2=buf, _co=c: e.tensor_reduce(
                wcols[:, _co:_co + 1], _b2[:], axis=AX.X, op=OP.add,
                apply_absolute_value=True), inc=v2)
            am_cons[c] = (v2, v2.n)

    rope_v = {}
    SCm = {}
    EXm = {}
    T2m = {}
    E1m = {}
    E2m = {}
    PVm = {}
    AEm = {}
    SLm = {}

    for h in range(16):
        parh = h % 2
        if (h - 2, 7) in PVm:
            b.w("y", t1, PVm[(h - 2, 7)])
        b.dma("y", kraw[parh][:], d_kf[h], ldy)
        b.dma("y", qraw[parh][:], d_qf[h], ldy)
        vsrc = d_vsp[h // 4, :, :, (h % 4) * 128:(h % 4 + 1) * 128].transpose([1, 0, 2])
        b.dma("y", vsb[parh][:], vsrc, ldy)
        KQV_LD = ldy.n
        b.w("y", ldy, KQV_LD)
        if h >= 1:
            b.w("y", v2, rope_v[h - 1])
        b.dma("y", kswap[0:64, :], kraw[parh][64:128, :], ldy)
        b.dma("y", kswap[64:128, :], kraw[parh][0:64, :], ldy)
        b.dma("y", qswap[0:64, :], qraw[parh][64:128, :], ldy)
        b.dma("y", qswap[64:128, :], qraw[parh][0:64, :], ldy)
        SW_LD = ldy.n

        b.w("v", ldy, SW_LD)
        if (h - 2, 7) in SCm:
            b.w("v", t1, SCm[(h - 2, 7)])
        b.add("v", lambda e, _p=parh, _h=h: e.scalar_tensor_tensor(m1k[:], kraw[_p][:], nw[:, 16 + _h:17 + _h], cosrs_k[:], op0=OP.mult, op1=OP.mult), inc=v2)
        b.add("v", lambda e, _h=h: e.scalar_tensor_tensor(m2k[:], kswap[:], nw[:, 48 + _h:49 + _h], sinrs_k[:], op0=OP.mult, op1=OP.mult), inc=v2)
        b.w("v", v2)
        b.add("v", lambda e, _p=parh: e.tensor_add(krot[_p][:], m1k[:], m2k[:]), inc=v2)
        b.add("v", lambda e, _p=parh, _h=h: e.scalar_tensor_tensor(m1q[:], qraw[_p][:], nw[:, 0 + _h:1 + _h], cosrs_q[:], op0=OP.mult, op1=OP.mult), inc=v2)
        b.add("v", lambda e, _h=h: e.scalar_tensor_tensor(m2q[:], qswap[:], nw[:, 32 + _h:33 + _h], sinrs_q[:], op0=OP.mult, op1=OP.mult), inc=v2)
        b.w("v", v2)
        b.add("v", lambda e, _p=parh: e.tensor_add(qrot[_p][:], m1q[:], m2q[:]), inc=v2)
        rope_v[h] = v2.n
        RON = v2.n
        if DEBUG_SPILLS and h in (8, 9):
            b.w("y", v2, RON)
            b.dma("y", d_kro[h - 8], krot[parh][:], st)
            b.dma("y", d_qro[h - 8], qrot[parh][:], st)

        for qt in range(8):
            par = qt % 2
            i4 = qt % 4
            u = h * 8 + qt
            base = par * 8
            prev = (h, qt - 2) if qt >= 2 else (h - 1, 6 + qt)
            prev4 = (h, qt - 4) if qt >= 4 else (h - 1, 4 + qt)

            for c in (range((u - 2) * 3, (u - 2) * 3 + 3) if u >= 2 else []):
                am_consume(c)
            for c in range(u * 3, u * 3 + 3):
                if c < 384:
                    am_issue(c)

            b.w("t", v2, RON)
            if prev in EXm:
                b.w("t", s1, EXm[prev])
            if u == 0:
                b.w("t", ld, MASK_LD)

            def mk_sc(_par=par, _qt=qt, _ph=parh):
                def fn(e):
                    ps = ps3[_par]
                    for sg, (c0, cw) in enumerate(((0, 512), (512, 512), (1024, 128))):
                        e.matmul(ps[:, c0:c0 + cw], qrot[_ph][:, _qt * 128:(_qt + 1) * 128],
                                 krot[_ph][:, _qt * 128 + c0:_qt * 128 + c0 + cw], start=True, stop=False)
                        ins = e.matmul(ps[:, c0:c0 + cw], identb[:],
                                       maskL[:, (_qt * 9) * 128 + c0:(_qt * 9) * 128 + c0 + cw],
                                       start=False, stop=True)
                    return ins
                return fn
            b.add("t", mk_sc(), inc=t1)
            SCm[(h, qt)] = t1.n

            b.w("v", t1, SCm[(h, qt)])
            b.add("v", lambda e, _par=par, _base=base: e.tensor_reduce(
                nms[:, _base + 3:_base + 4], ps3[_par][:, 0:1152], axis=AX.X, op=OP.max, negate=True), inc=v1)
            NMn = v1.n

            b.w("s", v1, NMn)
            if prev4 in T2m:
                b.w("s", t1, T2m[prev4])

            def mk_ex(_par=par, _i4=i4, _base=base):
                def fn(e):
                    return e.activation(ptil[_i4][:, 0:1152], ps3[_par][:, 0:1152], AF.Exp,
                                        bias=nms[:, _base + 3:_base + 4], accum_out=nms[:, _base + 4:_base + 5])
                return fn
            b.add("s", mk_ex(), inc=s1)
            EXm[(h, qt)] = s1.n

            b.w("v", s1, EXm[(h, qt)])
            b.add("v", lambda e, _base=base: e.reciprocal(nms[:, _base + 7:_base + 8], nms[:, _base + 4:_base + 5]), inc=v1)
            b.w("v", v1)
            b.add("v", lambda e, _i4=i4, _base=base: e.tensor_scalar(
                ptil[_i4][:], ptil[_i4][:], nms[:, _base + 7:_base + 8], None, op0=OP.mult), inc=v1)
            SLm[(h, qt)] = v1.n

            b.w("t", v1, SLm[(h, qt)])

            def mk_t1(_par=par, _i4=i4):
                def fn(e):
                    ins = None
                    for j in range(8):
                        ins = e.transpose(pbf[_par][:, j * 128:(j + 1) * 128],
                                          ptil[_i4][:, j * 128:(j + 1) * 128], identb[:])
                    return ins
                return fn
            b.add("t", mk_t1(), inc=t1)
            T1n = t1.n
            b.w("s", t1, T1n)
            b.add("s", lambda e, _par=par: e.copy(ptT[_par][:, 0:1024], pbf[_par][:, 0:1024]), inc=s1)
            E1m[(h, qt)] = s1.n
            b.w("t", s1, E1m[(h, qt)])
            b.add("t", lambda e, _par=par, _i4=i4: e.transpose(pbf[_par][:, 0:128], ptil[_i4][:, 1024:1152], identb[:]), inc=t1)
            T2m[(h, qt)] = t1.n
            b.w("s", t1, T2m[(h, qt)])
            b.add("s", lambda e, _par=par: e.copy(ptT[_par][:, 1024:1152], pbf[_par][:, 0:128]), inc=s1)
            E2m[(h, qt)] = s1.n

            b.w("t", s1, E2m[(h, qt)])

            def mk_pv(_par=par, _qt=qt, _ph=parh):
                def fn(e):
                    ins = None
                    for bp in range(9):
                        kt = _qt + bp
                        ins = e.matmul(ps3[_par][:, 1152:1280], vsb[_ph][:, kt * 128:(kt + 1) * 128],
                                       ptT[_par][:, bp * 128:(bp + 1) * 128],
                                       start=(bp == 0), stop=(bp == 8))
                    return ins
                return fn
            b.add("t", mk_pv(), inc=t1)
            PVm[(h, qt)] = t1.n

            b.w("s", t1, PVm[(h, qt)])
            b.add("s", lambda e, _par=par, _h=h, _qt=qt: e.copy(
                attn[_h][:, _qt * 128:(_qt + 1) * 128], ps3[_par][:, 1152:1280]), inc=s1)
            AEm[(h, qt)] = s1.n
    for c in range(126 * 3, 384):
        am_consume(c)
    if DEBUG_SPILLS:
        b.w("y", s1, s1.n)
        for hh in range(16):
            b.dma("y", d_att[hh], attn[hh][:], st)
    b.play("p3")

    b = B(nc)
    b.fence(ALLSEMS)
    for wi in range(3):
        b.add("v", lambda e, _w=wi: e.tensor_reduce(
            aws[:, 3 + _w:4 + _w], wcols[:, _w * 128:(_w + 1) * 128], axis=AX.X, op=OP.add), inc=v1)
    b.w("t", v1)
    for wi in range(3):
        b.add("t", lambda e, _c=3 + wi: e.matmul(pp[0][0:1, 0:1], ones_f[:], aws[:, _c:_c + 1],
                                                 start=True, stop=True), inc=t1)
        b.w("s", t1)
        b.add("s", lambda e, _c=8 + wi: e.copy(aws[0:1, _c:_c + 1], pp[0][0:1, 0:1]), inc=s1)
        b.w("t", s1)
    b.add("t", lambda e: e.matmul(pp[0][:, 0:3], ones_rf[:], aws[0:1, 8:11], start=True, stop=True), inc=t1)
    b.w("v", t1)
    b.add("v", lambda e: e.tensor_scalar(aws[:, 3:6], pp[0][:, 0:3], float(1.0 / (8192.0 * 2048.0)), None, op0=OP.mult), inc=v1)
    AMFV = v1.n
    b.w("v", v1)
    b.add("v", lambda e: e.tensor_scalar(aws[:, 3:6], aws[:, 3:6], 1e-5, None, op0=OP.max), inc=v1)
    b.w("v", v1)
    b.add("v", lambda e: e.tensor_scalar(aws[:, 0:3], aws[:, 3:6], 0.5, None, op0=OP.mult), inc=v1)
    b.w("v", v1)
    b.add("v", lambda e: e.reciprocal(aws[:, 3:6], aws[:, 3:6]), inc=v1)

    wo_t = {}
    ev_v = {}
    xo_ld = {}
    for fc in range(4):
        buf = wowb[fc % 2]
        if fc >= 2:
            b.w("g", t1, wo_t[fc - 2])
        b.dma("g", buf[:], d_wow[fc], ld)
        ldn = ld.n
        for tokt in range(8):
            ppb = pp[tokt % 2]
            b.w("t", ld, ldn)
            if fc == 0 and tokt == 0:
                b.w("t", v1, AMFV)
            keyp = (fc, tokt - 2) if tokt >= 2 else (fc - 1, 6 + tokt)
            if keyp in ev_v:
                b.w("t", v1, ev_v[keyp])

            def mk(_b=buf, _p=ppb, _tt=tokt):
                def fn(e):
                    ins = None
                    for hh in range(16):
                        ins = e.matmul(_p[:], attn[hh][:, _tt * 128:(_tt + 1) * 128],
                                       _b[:, hh * 512:(hh + 1) * 512],
                                       start=(hh == 0), stop=(hh == 15))
                    return ins
                return fn
            b.add("t", mk(), inc=t1)
            b.w("v", t1)
            b.add("v", lambda e, _p=ppb, _tt=tokt, _fc=fc: e.tensor_copy(
                y1t[_tt][:, _fc * 512:(_fc + 1) * 512], _p[:]), inc=v1)
            ev_v[(fc, tokt)] = v1.n
        wo_t[fc] = t1.n

    WON = v1.n
    for tokt in range(8):
        if tokt >= 2:
            b.w("y", v1, xo_ld.get(tokt - 2, 0))
        b.dma("y", xownb[tokt % 2][:], d_xown[tokt * 128:(tokt + 1) * 128, :], ldy)
        XOL = ldy.n
        b.w("v", ldy, XOL)
        b.w("v", v1, WON)
        b.add("v", lambda e, _tt=tokt: e.tensor_add(y1t[_tt][:], y1t[_tt][:], xownb[_tt % 2][:]), inc=v1)
        xo_ld[tokt] = v1.n

    b.w("v", v1, v1.n)
    for tokt in range(8):
        b.add("v", lambda e, _t=tokt: e.tensor_reduce(stA[:, _t:_t + 1], y1t[_t][:], axis=AX.X, op=OP.add), inc=v1)
    b.w("s", v1)
    for tokt in range(8):
        b.add("s", lambda e, _t=tokt: e.activation(sqscr[:], y1t[_t][:], AF.Square, accum_out=stA[:, 8 + _t:9 + _t]), inc=s1)
    b.w("v", s1)
    b.add("v", lambda e: e.tensor_scalar(stA[:, 0:8], stA[:, 0:8], float(1.0 / 2048.0), None, op0=OP.mult), inc=v1)
    b.w("v", v1)
    b.add("v", lambda e: e.tensor_scalar(stA[:, 8:16], stA[:, 8:16], float(1.0 / 2048.0), None, op0=OP.mult), inc=v1)
    b.w("v", v1)
    b.add("v", lambda e: e.tensor_mul(stA[:, 56:64], stA[:, 0:8], stA[:, 0:8]), inc=v1)
    b.w("v", v1)
    b.add("v", lambda e: e.tensor_sub(stA[:, 8:16], stA[:, 8:16], stA[:, 56:64]), inc=v1)
    b.add("v", lambda e: e.memset(cst[:, 7:8], 1e-5), inc=v1)
    b.w("s", v1)
    b.add("s", lambda e: e.activation(stA[:, 16:24], stA[:, 8:16], AF.Sqrt, bias=cst[:, 7:8]), inc=s1)
    b.w("v", s1)
    b.add("v", lambda e: e.reciprocal(stA[:, 16:24], stA[:, 16:24]), inc=v1)
    b.w("v", v1)
    b.add("v", lambda e: e.tensor_mul(stA[:, 24:32], stA[:, 0:8], stA[:, 16:24]), inc=v1)
    b.w("v", v1)
    b.add("v", lambda e: e.tensor_scalar(stA[:, 24:32], stA[:, 24:32], -1.0, None, op0=OP.mult), inc=v1)
    LN1N = v1.n
    b.w("s", v1, LN1N)
    for tokt in range(8):
        b.add("s", lambda e, _t=tokt: e.activation(y1t[_t][:], y1t[_t][:], AF.Identity,
                                                   bias=stA[:, 24 + _t:25 + _t], scale=stA[:, 16 + _t:17 + _t]), inc=s1)
    X1N = s1.n
    b.w("y", s1, X1N)
    for tokt in range(8):
        b.dma("y", d_x1[tokt * 128:(tokt + 1) * 128, :], y1t[tokt][:], st)
    b.w("v", s1, X1N)
    for tokt in range(8):
        b.add("v", lambda e, _t=tokt: e.tensor_reduce(stA[:, 32 + _t:33 + _t], y1t[_t][:], axis=AX.X, op=OP.max,
                                                      apply_absolute_value=True), inc=v1)
    b.w("v", v1)
    b.add("v", lambda e: e.tensor_scalar(stA[:, 32:40], stA[:, 32:40], 1e-5, None, op0=OP.max), inc=v1)
    b.w("v", v1)
    b.add("v", lambda e: e.reciprocal(stA[:, 40:48], stA[:, 32:40]), inc=v1)
    b.w("v", v1)
    b.add("v", lambda e: e.tensor_scalar(stA[:, 40:48], stA[:, 40:48], 128.0, None, op0=OP.mult), inc=v1)
    b.w("v", v1)
    b.add("v", lambda e: e.tensor_scalar(stA[:, 48:56], stA[:, 32:40], float(1.0 / 128.0), None, op0=OP.mult), inc=v1)
    QSN = v1.n

    xq_rel = {}
    for tokt in range(8):
        b.w("s", v1, QSN)
        b.add("s", lambda e, _t=tokt: e.activation(sqscr[:], y1t[_t][:], AF.Identity, scale=stA[:, 40 + _t:41 + _t]), inc=s1)
        b.w("v", s1)
        b.add("v", lambda e: e.tensor_scalar(sqscr2[:], sqscr[:], cst[:, 4:5], None, op0=OP.add), inc=v1)
        b.w("v", v1)
        b.add("v", lambda e: e.tensor_scalar(sqscr[:], sqscr2[:], cst[:, 5:6], None, op0=OP.add), inc=v1)
        b.w("v", v1)
        if tokt >= 2:
            b.w("v", t1, xq_rel[tokt - 2])
        b.add("v", lambda e, _t=tokt: e.tensor_scalar(xqs[_t % 2][:], sqscr[:], 127.0, None, op0=OP.min), inc=v1)
        XQV = v1.n
        b.w("t", v1, XQV)
        for k in range(16):
            b.add("t", lambda e, _t=tokt, _k=k: e.transpose(
                pbf[0][:, 0:128], xqs[_t % 2][:, _k * 128:(_k + 1) * 128], identb[:]), inc=t1)
            b.w("s", t1)
            b.add("s", lambda e, _t=tokt, _k=k: e.copy(
                xqT[_k][:, _t * 128:(_t + 1) * 128], pbf[0][:, 0:128]), inc=s1)
            b.w("t", s1)
        xq_rel[tokt] = t1.n

    for tokt in range(8):
        b.add("t", lambda e, _t=tokt: e.transpose(pp[4][0:1, 0:128], stA[:, 48 + _t:49 + _t], identf[:]), inc=t1)
        b.w("s", t1)
        b.add("s", lambda e, _t=tokt: e.copy(arow[0:1, _t * 128:(_t + 1) * 128], pp[4][0:1, 0:128]), inc=s1)
        b.w("t", s1)
    b.w("t", s1)
    for ci in range(2):
        b.add("t", lambda e, _c=ci: e.matmul(pp[4][:, :], ones_rf[:], arow[0:1, _c * 512:(_c + 1) * 512],
                                             start=True, stop=True), inc=t1)
        b.w("s", t1)
        b.add("s", lambda e, _c=ci: e.copy(amcb[:, _c * 512:(_c + 1) * 512], pp[4][:, :]), inc=s1)
        b.w("t", s1)
    b.w("v", s1)
    b.add("v", lambda e: e.tensor_scalar(cgb[:], amcb[:], aws[:, 0:1], None, op0=OP.mult), inc=v1)
    b.add("v", lambda e: e.tensor_scalar(cub[:], amcb[:], aws[:, 1:2], None, op0=OP.mult), inc=v1)
    b.play("p4")

    b = B(nc)
    b.fence(ALLSEMS)
    q_s = {}
    m_t = {}
    ad_v = {}
    sil_s = {}
    ev2 = {}
    sp_y = {}
    bank_rel = [0] * 6
    for ft in range(64):
        p2_ = ft % 2
        if ft >= 2:
            b.w("g", s1, q_s[ft - 2])
        b.dma("g", gwf[p2_][:], d_gw[ft], ld)
        b.dma("g", uwf[p2_][:], d_uw[ft], ld)
        ldn = ld.n
        b.w("s", ld, ldn)
        if ft >= 2:
            b.w("s", v1, ad_v[ft - 2])
        b.add("s", lambda e, _p=p2_: e.activation(tsgg1[_p][:], gwf[_p][:], AF.Sign, bias=cst[:, 6:7], scale=aws[:, 3:4]), inc=s1)
        b.add("s", lambda e, _p=p2_: e.activation(tsgg2[_p][:], gwf[_p][:], AF.Sign, bias=cst[:, 14:15], scale=aws[:, 3:4]), inc=s1)
        b.add("s", lambda e, _p=p2_: e.activation(tsgu1[_p][:], uwf[_p][:], AF.Sign, bias=cst[:, 6:7], scale=aws[:, 4:5]), inc=s1)
        b.add("s", lambda e, _p=p2_: e.activation(tsgu2[_p][:], uwf[_p][:], AF.Sign, bias=cst[:, 14:15], scale=aws[:, 4:5]), inc=s1)
        q_s[ft] = s1.n
        b.w("v", s1, q_s[ft])
        if ft >= 2:
            b.w("v", t1, m_t[ft - 2])
        b.add("v", lambda e, _p=p2_: e.tensor_add(wqg[_p][:], tsgg1[_p][:], tsgg2[_p][:]), inc=v1)
        b.add("v", lambda e, _p=p2_: e.tensor_add(wqu[_p][:], tsgu1[_p][:], tsgu2[_p][:]), inc=v1)
        ad_v[ft] = v1.n

        def post(pf):
            b.w("v", t1, m_t[pf])
            if pf >= 1:
                b.w("v", s1, sil_s[pf - 1])
            for j in range(4):
                bk = (pf * 4 + j) % 6
                ch = j % 2
                dst = tgv if j < 2 else tuv
                csb = cgb if j < 2 else cub
                b.add("v", lambda e, _d=dst, _p=pp[bk], _c=csb, _ch=ch: e.tensor_mul(
                    _d[:, _ch * 512:(_ch + 1) * 512], _p[:], _c[:, _ch * 512:(_ch + 1) * 512]), inc=v1)
                bank_rel[bk] = v1.n
            b.w("s", v1)
            b.add("s", lambda e: e.activation(tgs[:], tgv[:], AF.Silu), inc=s1)
            sil_s[pf] = s1.n
            b.w("v", s1)
            if pf >= 2:
                b.w("v", st, sp_y[pf - 2])
            b.add("v", lambda e, _f=pf: e.tensor_mul(intev[_f % 2][:], tgs[:], tuv[:]), inc=v2)
            ev2[pf] = v2.n
            b.w("y", v2)
            b.dma("y", d_int[pf], intev[pf % 2][:], st)
            sp_y[pf] = st.n

        if ft >= 1:
            post(ft - 1)

        b.w("t", v1, ad_v[ft])
        for j in range(4):
            bk = (ft * 4 + j) % 6
            wq = wqg[p2_] if j < 2 else wqu[p2_]
            ch = j % 2
            if bank_rel[bk]:
                b.w("t", v1, bank_rel[bk])

            def mk(_w=wq, _p=pp[bk], _ch=ch):
                def fn(e):
                    ins = None
                    for kt in range(16):
                        ins = e.matmul(_p[:], _w[:, kt * 128:(kt + 1) * 128],
                                       xqT[kt][:, _ch * 512:(_ch + 1) * 512],
                                       start=(kt == 0), stop=(kt == 15))
                    return ins
                return fn
            b.add("t", mk(), inc=t1)
        m_t[ft] = t1.n
    post(63)
    b.play("p5")

    b = B(nc)
    b.fence(ALLSEMS)
    QMUL = {}
    QS1 = {}
    QS3 = {}
    SPL6 = {}
    DT_SG = {}
    DT_AD = {}
    DT_SP = {}
    STATRD = 0
    SBCZn = 0
    for h2 in range(2):
        lo = h2 * 512
        for k in range(64):
            if h2 == 1:
                lk = log_of_phys.get(id(zw[k]))
                if lk is not None:
                    b.w("g", s1, QS1[(0, lk)])
                else:
                    b.w("g", v1, BFN0)
            b.dma("g", zw[k][:], d_int[k, :, lo:lo + 512], ld)
        LDALL = ld.n
        b.w("v", ld, LDALL)
        if h2 == 1:
            for j in BSCR_LOG:
                b.w("v", s1, QS1[(0, j)])
        tl = list(zw)
        sc = list(bscr)
        sched = []
        for stg_i in range(6):
            stride = 1 << stg_i
            si = 0
            for base2 in range(0, 64, stride * 2):
                for off2 in range(stride):
                    a2 = base2 + off2
                    c2 = a2 + stride
                    s_t = sc[si % 4]
                    sched.append((s_t, tl[a2], tl[c2]))
                    sc[si % 4] = tl[c2]
                    tl[c2] = s_t
                    si += 1

        def mk_b(_sched=list(sched)):
            def fn(e):
                ins = None
                for s_t, a_t, c_t in _sched:
                    e.tensor_sub(s_t[:], a_t[:], c_t[:])
                    ins = e.tensor_add(a_t[:], a_t[:], c_t[:])
                return ins
            return fn
        b.add("v", mk_b(), inc=v1)
        BFN = v1.n
        if h2 == 0:
            BFN0 = BFN
        ztl = list(tl)
        if h2 == 0:
            log_of_phys = {id(hd): j for j, hd in enumerate(ztl)}

        b.w("t", v1, BFN)
        if h2 == 1:
            b.w("t", v1, STATRD)
            b.w("t", s1, SBCZn)
        bankev = getattr(b, "_bankev", None) or [0] * 6
        for k in range(64):
            ppb = pp[k % 6]
            if bankev[k % 6]:
                b.w("t", s1, bankev[k % 6])
            b.add("t", lambda e, _z=ztl[k], _p=ppb: e.matmul(_p[:], h128f[:], _z[:], start=True, stop=True), inc=t1)
            b.w("s", t1)
            b.add("s", lambda e, _p=ppb, _z=ztl[k]: e.copy(_z[:], _p[:]), inc=s1)
            bankev[k % 6] = s1.n
            b.w("v", s1)
            if k == 0:
                b.add("v", lambda e, _z=ztl[0]: e.tensor_copy(mxz[:], _z[:]))
                b.add("v", lambda e, _z=ztl[0]: e.tensor_copy(mnz[:], _z[:]), inc=v1)
            else:
                b.add("v", lambda e, _z=ztl[k]: e.tensor_max(mxz[:], mxz[:], _z[:]))
                b.add("v", lambda e, _z=ztl[k]: e.tensor_tensor(mnz[:], mnz[:], _z[:], op=OP.min), inc=v1)
        b._bankev = bankev
        b.w("v", v1)
        b.add("v", lambda e: e.tensor_scalar(absz[:], mnz[:], -1.0, None, op0=OP.mult), inc=v1)
        b.w("v", v1)
        b.add("v", lambda e: e.tensor_max(absz[:], absz[:], mxz[:]), inc=v1)
        ABSN = v1.n
        b.w("t", v1, ABSN)
        for j in range(4):
            b.add("t", lambda e, _j=j: e.transpose(pp[_j][:, 0:128], absz[:, _j * 128:(_j + 1) * 128], identf[:]), inc=t1)
        TRN = t1.n
        b.w("v", t1, TRN)
        for j in range(4):
            b.add("v", lambda e, _j=j, _h=h2: e.tensor_reduce(
                stB[:, 32 + 4 * _h + _j:33 + 4 * _h + _j], pp[_j][:, 0:128], axis=AX.X, op=OP.max), inc=v1)
        STATRD = v1.n
        cA, cB, cC = 32 + 4 * h2, 0 + 4 * h2, 8 + 4 * h2
        b.w("v", v1)
        b.add("v", lambda e, _a=cA, _b2=cB: e.tensor_scalar(stB[:, _b2:_b2 + 4], stB[:, _a:_a + 4], INV_N_FF, None, op0=OP.mult), inc=v1)
        if DEBUG_SPILLS and h2 == 0:
            b.w("y", v1, v1.n)
            b.dma("y", d_st[:, 576:640], stB[:], st)
            b.w("v", st, st.n)
        b.w("v", v1)
        b.add("v", lambda e, _b2=cB: e.tensor_scalar(stB[:, _b2:_b2 + 4], stB[:, _b2:_b2 + 4], 1e-5, None, op0=OP.max), inc=v1)
        b.w("v", v1)
        b.add("v", lambda e, _b2=cB, _c=cC: e.reciprocal(stB[:, _c:_c + 4], stB[:, _b2:_b2 + 4]), inc=v1)
        b.w("v", v1)
        b.add("v", lambda e, _c=cC: e.tensor_scalar(stB[:, _c:_c + 4], stB[:, _c:_c + 4], float(128.0 * np.float64(INV_N_FF)), None, op0=OP.mult), inc=v1)
        b.w("v", v1)
        b.add("v", lambda e, _b2=cB, _o=48 + 4 * h2: e.tensor_scalar(stB[:, _o:_o + 4], stB[:, _b2:_b2 + 4], float(1.0 / 128.0), None, op0=OP.mult), inc=v1)
        b.w("v", v1)
        b.add("v", lambda e, _o=48 + 4 * h2: e.tensor_scalar(stB[:, _o:_o + 4], stB[:, _o:_o + 4], aws[:, 2:3], None, op0=OP.mult), inc=v1)
        SCN = v1.n
        if DEBUG_SPILLS and h2 == 0:
            b.w("y", v1, SCN)
            b.dma("y", d_st[:, 640:704], stB[:], st)
            b.w("v", st, st.n)
            b.w("t", st, st.n)
        b.w("t", v1, SCN)
        for j in range(4):
            b.add("t", lambda e, _j=j, _c=cC: e.transpose(
                pp[4][0:1, _j * 128:(_j + 1) * 128], stB[:, _c + _j:_c + _j + 1], identf[:]), inc=t1)
        b.w("s", t1)
        b.add("s", lambda e: e.copy(arow2[0:1, 0:512], pp[4][0:1, 0:512]), inc=s1)
        b.w("t", s1)
        b.add("t", lambda e: e.matmul(pp[5][:], ones_rf[:], arow2[0:1, 0:512], start=True, stop=True), inc=t1)
        b.w("s", t1)
        b.add("s", lambda e: e.copy(sbcz[:], pp[5][:]), inc=s1)
        SBCZn = s1.n
        if DEBUG_SPILLS and h2 == 0:
            b.w("y", s1, SBCZn)
            b.dma("y", d_st[:, 0:64], stB[:], st)
            b.dma("y", d_st[:, 64:576], sbcz[:], st)

        bscr_ids = {id(x) for x in bscr}
        korder = ([j for j in range(64) if id(ztl[j]) in bscr_ids]
                  + [j for j in range(64) if id(ztl[j]) not in bscr_ids])
        if h2 == 0:
            BSCR_LOG = [j for j in range(64) if id(ztl[j]) in bscr_ids]
        for pos, k in enumerate(korder):
            c3, c2w, c4 = pos % 3, pos % 2, pos % 4
            if pos == 0:
                b.w("v", s1, SBCZn)
            b.add("v", lambda e, _z=ztl[k]: e.tensor_mul(_z[:], _z[:], sbcz[:]), inc=v1)
            QMUL[(h2, k)] = v1.n
            b.w("s", v1, QMUL[(h2, k)])
            if (h2, pos - 4) in SPL6:
                b.w("s", st, SPL6[(h2, pos - 4)])
            b.add("s", lambda e, _z=ztl[k], _c2=c2w: e.activation(zsc2[_c2][:], _z[:], AF.Identity, bias=cst[:, 4:5]), inc=s1)
            QS1[(h2, k)] = s1.n
            b.add("s", lambda e, _c3=c3, _c2=c2w: e.activation(zsc[_c3][:], zsc2[_c2][:], AF.Relu, bias=cst[:, 3:4], scale=cst[:, 8:9]), inc=s1)
            b.add("s", lambda e, _c3=c3, _c4=c4: e.activation(zqe[_c4][:], zsc[_c3][:], AF.Identity, bias=cst[:, 2:3], scale=cst[:, 8:9]), inc=s1)
            QS3[(h2, pos)] = s1.n
            b.w("y", s1, QS3[(h2, pos)])
            b.dma("y", d_zq[k][:, lo:lo + 512], zqe[c4][:], st)
            SPL6[(h2, pos)] = st.n

            if h2 == 0:
                dt = pos
                for hf in range(2):
                    if dt >= 1:
                        b.w("g", s2, DT_SG[(dt - 1, hf)])
                    b.dma("g", dwf6h[hf][:], d_dw[dt // 16, dt % 16][:, hf * 1024:(hf + 1) * 1024], ld)
                    dt_ld = ld.n
                    b.w("s", ld, dt_ld)
                    if hf == 0 and dt >= 1:
                        b.w("s", v2, DT_AD.get(dt - 1, 0))
                    b.add("s", lambda e, _b2=dwf6h[hf], _hf=hf: e.activation(
                        td61[:, _hf * 1024:(_hf + 1) * 1024], _b2[:], AF.Sign, bias=cst[:, 6:7], scale=aws[:, 5:6]), inc=s2)
                    b.add("s", lambda e, _b2=dwf6h[hf], _hf=hf: e.activation(
                        td62[:, _hf * 1024:(_hf + 1) * 1024], _b2[:], AF.Sign, bias=cst[:, 14:15], scale=aws[:, 5:6]), inc=s2)
                    DT_SG[(dt, hf)] = s2.n
                    if hf == 1:
                        b.w("v", s2, DT_SG[(dt, 1)])
                        if dt >= 2:
                            b.w("v", st, DT_SP[dt - 2])
                        b.add("v", lambda e, _i=dt % 2: e.tensor_add(dwq6[_i][:], td61[:], td62[:]), inc=v2)
                        DT_AD[dt] = v2.n
                        b.w("y", v2, DT_AD[dt])
                        b.dma("y", d_dwq[dt], dwq6[dt % 2][:], st)
                        DT_SP[dt] = st.n
    b.play("p6")

    b = B(nc)
    b.fence(ALLSEMS)
    b.add("v", lambda e: e.memset(stB[:, 0:16], 0.0), inc=v1)
    dn_m = {}
    dnm_list = []
    ev7 = {}
    SPL7 = {}
    for grp in range(2):
        if grp == 1:
            b.w("g", t1, dn_m[(0, 3, 15)])
        for k in range(64):
            b.dma("g", zq7[k][:], d_zq[k][:, grp * 512:(grp + 1) * 512], ld)
        ZQLD = ld.n
        for fo in range(4):
            for kc in range(16):
                gkey = (grp, fo, kc)
                if len(dnm_list) >= 4:
                    b.w("g", t1, dnm_list[-4])
                b.dma("g", dwq4[(fo * 16 + kc) % 4][:], d_dwq[fo * 16 + kc], ld)
                ldn = ld.n
                b.w("t", ld, ldn)
                if fo == 0 and kc == 0:
                    b.w("t", ld, ZQLD)
                if kc == 0:
                    pe = (grp, fo - 1) if fo >= 1 else (grp - 1, 3)
                    if pe in ev7:
                        b.w("t", v1, ev7[pe])

                def mk(_w=dwq4[(fo * 16 + kc) % 4], _kc=kc):
                    def fn(e):
                        ins = None
                        for k4 in range(4):
                            kt = _kc * 4 + k4
                            for ti in range(4):
                                ins = e.matmul(pp[ti][:],
                                               zq7[kt][:, ti * 128:(ti + 1) * 128],
                                               _w[:, k4 * 512:(k4 + 1) * 512],
                                               start=(kt == 0), stop=(kt == 63))
                        return ins
                    return fn
                b.add("t", mk(), inc=t1)
                dn_m[gkey] = t1.n
                dnm_list.append(t1.n)
            DMN = t1.n
            for ti in range(4):
                tokt = grp * 4 + ti
                b.dma("y", x1rd[ti % 2][:], d_x1[tokt * 128:(tokt + 1) * 128, fo * 512:(fo + 1) * 512], ldy)
                ldyn = ldy.n
                b.w("v", t1, DMN)
                b.w("v", ldy, ldyn)
                pk = (grp, fo, ti - 2)
                if ti < 2:
                    pk = (grp, fo - 1, ti + 2) if fo >= 1 else (grp - 1, 3, ti + 2)
                if pk in SPL7:
                    b.w("v", st, SPL7[pk])
                b.add("v", lambda e, _ti=ti, _tt=tokt: e.scalar_tensor_tensor(
                    yev[_ti % 2][:], pp[_ti][:], stB[:, 48 + _tt:49 + _tt], x1rd[_ti % 2][:],
                    op0=OP.mult, op1=OP.add), inc=v1)
                YEN = v1.n
                b.w("v", v1, YEN)
                b.add("v", lambda e, _ti=ti: e.tensor_reduce(aws[:, 6:7], yev[_ti % 2][:], axis=AX.X, op=OP.add), inc=v1)
                b.w("v", v1)
                b.add("v", lambda e, _tt=tokt: e.tensor_add(stB[:, _tt:_tt + 1], stB[:, _tt:_tt + 1], aws[:, 6:7]), inc=v1)
                b.w("s", v1, YEN)
                b.add("s", lambda e, _ti=ti: e.activation(sqs7[:], yev[_ti % 2][:], AF.Square, accum_out=aws[:, 7:8]), inc=s1)
                b.w("v", s1)
                b.add("v", lambda e, _tt=tokt: e.tensor_add(stB[:, 8 + _tt:9 + _tt], stB[:, 8 + _tt:9 + _tt], aws[:, 7:8]), inc=v1)
                b.w("y", v1)
                b.dma("y", d_y[tokt * 128:(tokt + 1) * 128, fo * 512:(fo + 1) * 512], yev[ti % 2][:], st)
                SPL7[(grp, fo, ti)] = st.n
            ev7[(grp, fo)] = v1.n
    YALL = st.n

    b.w("v", v1)
    b.add("v", lambda e: e.tensor_scalar(stB[:, 0:8], stB[:, 0:8], float(1.0 / 2048.0), None, op0=OP.mult), inc=v1)
    b.w("v", v1)
    b.add("v", lambda e: e.tensor_scalar(stB[:, 8:16], stB[:, 8:16], float(1.0 / 2048.0), None, op0=OP.mult), inc=v1)
    b.w("v", v1)
    b.add("v", lambda e: e.tensor_mul(stB[:, 56:64], stB[:, 0:8], stB[:, 0:8]), inc=v1)
    b.w("v", v1)
    b.add("v", lambda e: e.tensor_sub(stB[:, 8:16], stB[:, 8:16], stB[:, 56:64]), inc=v1)
    b.w("s", v1)
    b.add("s", lambda e: e.activation(stB[:, 16:24], stB[:, 8:16], AF.Sqrt, bias=cst[:, 7:8]), inc=s1)
    b.w("v", s1)
    b.add("v", lambda e: e.reciprocal(stB[:, 16:24], stB[:, 16:24]), inc=v1)
    b.w("v", v1)
    b.add("v", lambda e: e.tensor_mul(stB[:, 24:32], stB[:, 0:8], stB[:, 16:24]), inc=v1)
    b.w("v", v1)
    b.add("v", lambda e: e.tensor_scalar(stB[:, 24:32], stB[:, 24:32], -1.0, None, op0=OP.mult), inc=v1)
    LN2N = v1.n

    outs = {}
    b.w("y", st, YALL)
    for tokt in range(8):
        b.dma("y", yfin[tokt % 2][:], d_y[tokt * 128:(tokt + 1) * 128, :], ldy)
        ldyn = ldy.n
        b.w("s", ldy, ldyn)
        b.w("s", v1, LN2N)
        if tokt >= 2:
            b.w("s", st, outs[tokt - 2])
        b.add("s", lambda e, _t=tokt: e.activation(ofin[_t % 2][:], yfin[_t % 2][:], AF.Identity,
                                                   bias=stB[:, 24 + _t:25 + _t], scale=stB[:, 16 + _t:17 + _t]), inc=s1)
        b.w("y", s1)
        b.dma("y", d_out[tokt * 128:(tokt + 1) * 128, :], ofin[tokt % 2][:], st)
        outs[tokt] = st.n
    b.w("y", st, st.n)
    b.play("p7")

    return nc


def _host_prep(x, wqkv, qnw, knw, wo, wg, wu, wd):
    f32 = np.float32
    bf = ml_dtypes.bfloat16
    x = np.asarray(x, f32)
    wqkv = np.asarray(wqkv, f32)
    qnw = np.asarray(qnw, f32)
    knw = np.asarray(knw, f32)
    wo = np.asarray(wo, f32)
    wg = np.asarray(wg, f32)
    wu = np.asarray(wu, f32)
    wd = np.asarray(wd, f32)

    qkw = np.ascontiguousarray(wqkv[:4096].reshape(32, 128, 16, 128).transpose(0, 3, 2, 1)).reshape(32, 128, 2048)
    vw = np.ascontiguousarray(wqkv[4096:].reshape(4, 512, 16, 128).transpose(0, 3, 2, 1)).reshape(4, 128, 8192)
    wow = np.ascontiguousarray(wo.reshape(4, 512, 16, 128).transpose(0, 3, 2, 1)).reshape(4, 128, 8192)
    gwa = np.ascontiguousarray(wg.reshape(64, 128, 16, 128).transpose(0, 3, 2, 1)).reshape(64, 128, 2048)
    uwa = np.ascontiguousarray(wu.reshape(64, 128, 16, 128).transpose(0, 3, 2, 1)).reshape(64, 128, 2048)
    dwa = np.ascontiguousarray(wd.reshape(4, 512, 64, 128).transpose(0, 2, 3, 1))
    dwa = np.ascontiguousarray(dwa.reshape(4, 16, 4, 128, 512).transpose(0, 1, 3, 2, 4)).reshape(4, 16, 128, 2048)

    nwarr = np.zeros((128, 64), f32)
    qn2 = qnw.reshape(16, 128)
    kn2 = knw.reshape(16, 128)
    nwarr[:, 0:16] = qn2.T
    nwarr[:, 16:32] = kn2.T
    nwarr[:, 32:48] = np.roll(qn2, -64, axis=1).T
    nwarr[:, 48:64] = np.roll(kn2, -64, axis=1).T

    identbm = np.eye(128, dtype=bf)
    identfm = np.eye(128, dtype=f32)
    iii, jjj = np.meshgrid(np.arange(128), np.arange(128), indexing="ij")
    pc = np.vectorize(lambda a: bin(int(a)).count("1"))(iii & jjj)
    h128m = ((-1.0) ** pc).astype(bf)
    h128fm = ((-1.0) ** pc).astype(f32)
    ii = np.arange(128)
    miscm = np.zeros((128, 2), f32)
    miscm[:, 0] = (ii % 64) / 64.0
    miscm[:, 1] = np.where(ii < 64, -1.0, 1.0)

    invf64 = 10000.0 ** (-(np.arange(64) / 64.0))

    ins = []
    for c in range(8):
        bb, j = c // 4, c % 4
        xo = np.ascontiguousarray(x[bb, j * 1024:(j + 1) * 1024])
        xp = np.ascontiguousarray(x[bb, (j - 1) * 1024:j * 1024]) if j >= 1 else xo
        xTarr = np.ascontiguousarray(np.concatenate([xp, xo], 0).T)

        if j == 0:
            pos = np.concatenate([np.arange(1024), np.arange(1024)]).astype(np.float64)
        else:
            pos = ((j - 1) * 1024 + np.arange(2048)).astype(np.float64)
        sgn = np.where(ii < 64, -1.0, 1.0)
        freq = sgn[:, None] * invf64[ii % 64][:, None] * pos[None, :]
        ksin = np.rint(freq / (2 * np.pi)).astype(f32)
        kcos = (np.rint((freq + np.pi / 2) / (2 * np.pi)) - 0.25).astype(f32)
        posb = np.broadcast_to(pos.astype(f32), (128, 2048)).copy()

        mask = np.zeros((8, 9, 128, 128), np.float32)
        kk = np.arange(128)
        qq = np.arange(128)
        for qt in range(8):
            for bp in range(9):
                jb = qt + bp
                kb = jb * 128 + kk[None, :]
                ql = qt * 128 + qq[:, None]
                valid = (kb >= ql + 1) & (kb <= ql + 1024)
                if j == 0:
                    valid = valid & (kb >= 1024)
                mask[qt, bp] = np.where(valid, 0.0, NEGBIG)
        maskLa = np.ascontiguousarray(mask.transpose(2, 0, 1, 3)).reshape(128, 72 * 128).astype(bf)

        ins.append({
            "xT": xTarr, "xown": xo, "qkw": qkw, "vw": vw, "wow": wow,
            "gw": gwa, "uw": uwa, "dw": dwa, "nw": nwarr, "posb": posb,
            "ksin": ksin, "kcos": kcos, "maskL": maskLa,
            "identb": identbm, "identf": identfm, "h128": h128m, "h128f": h128fm,
            "miscc": miscm,
        })
    return ins


_NC_CACHE = None


def kernel(L0_attn_hidden_states, L0_attn_w_qkv, L0_attn_q_norm_w, L0_attn_k_norm_w,
           L0_attn_w_o, L0_ffn_w_gate, L0_ffn_w_up, L0_ffn_w_down):
    global _NC_CACHE
    ins = _host_prep(L0_attn_hidden_states, L0_attn_w_qkv, L0_attn_q_norm_w,
                     L0_attn_k_norm_w, L0_attn_w_o, L0_ffn_w_gate, L0_ffn_w_up,
                     L0_ffn_w_down)
    if _NC_CACHE is None:
        _NC_CACHE = build_nc()
    res = run_bass_kernel_spmd(_NC_CACHE, ins, core_ids=list(range(8)))
    out = np.zeros((2, 4096, 2048), np.float32)
    for c in range(8):
        bb, j = c // 4, c % 4
        out[bb, j * 1024:(j + 1) * 1024] = res.results[c]["out"]
    return out


# version 19 (final)
# speedup vs baseline: 1.0350x; 1.0350x over previous
import numpy as np
import ml_dtypes

import concourse.bass as bass
import concourse.mybir as mybir
from concourse.bass_utils import run_bass_kernel_spmd

F32 = mybir.dt.float32
BF16 = mybir.dt.bfloat16
F16 = mybir.dt.float16
AF = mybir.ActivationFunctionType
OP = mybir.AluOpType
AX = mybir.AxisListType

H = 2048
S_OWN = 1024
S_KV = 2048
DEBUG_SPILLS = False
C23 = float(np.float32(1.5 * 2.0 ** 23))
C52 = float(np.float32(1.5 * 2.0 ** 52))
INV_N_FF = float(np.float32(1.0 / np.sqrt(8192.0)))
NEGBIG = -30000.0
TWO_PI = 2.0 * np.pi
CW1 = float(np.float32(6.28125))
_c2 = np.float64(np.float32(TWO_PI - 6.28125))
CW2 = float(np.float32(_c2 - (_c2 % np.float64(2.0 ** -22))))
CW3 = float(np.float32(TWO_PI - CW1 - CW2))


class Sem:
    __slots__ = ("h", "n")

    def __init__(self, nc, name):
        self.h = nc.alloc_semaphore(name)
        self.n = 0


class B:
    def __init__(self, nc):
        self.nc = nc
        self.o = {k: [] for k in "tvsgy"}

    def w(self, eng, sem, val=None):
        v = sem.n if val is None else val
        if v > 0:
            self.o[eng].append(lambda e, _s=sem.h, _v=v: e.wait_ge(_s, _v))

    def add(self, eng, fn, inc=None, n=1):
        if inc is None:
            self.o[eng].append(fn)
        else:
            inc.n += n
            self.o[eng].append(lambda e, _f=fn, _s=inc.h, _n=n: _f(e).then_inc(_s, _n))

    def fence(self, sems):
        for eng in "tvsgy":
            for sm in sems:
                self.w(eng, sm)

    def dma(self, eng, out_ap, in_ap, inc):
        self.add(eng, lambda e, _o=out_ap, _i=in_ap: e.dma_start(_o, _i), inc=inc, n=16)

    def play(self, blockname):
        with self.nc.Block(blockname) as blk:
            @blk.tensor
            def _(e):
                for f in self.o["t"]:
                    f(e)

            @blk.vector
            def _(e):
                for f in self.o["v"]:
                    f(e)

            @blk.scalar
            def _(e):
                for f in self.o["s"]:
                    f(e)

            @blk.gpsimd
            def _(e):
                for f in self.o["g"]:
                    f(e)

            @blk.sync
            def _(e):
                for f in self.o["y"]:
                    f(e)


def build_nc():
    nc = bass.Bass("TRN2", target_bir_lowering=False, detect_race_conditions=False)

    d_xT = nc.dram_tensor("xT", [H, S_KV], F32, kind="ExternalInput")
    d_xown = nc.dram_tensor("xown", [S_OWN, H], F32, kind="ExternalInput")
    d_qkw = nc.dram_tensor("qkw", [32, 128, 2048], F32, kind="ExternalInput")
    d_vw = nc.dram_tensor("vw", [4, 128, 8192], F32, kind="ExternalInput")
    d_wow = nc.dram_tensor("wow", [4, 128, 8192], F32, kind="ExternalInput")
    d_gw = nc.dram_tensor("gw", [64, 128, 2048], F32, kind="ExternalInput")
    d_uw = nc.dram_tensor("uw", [64, 128, 2048], F32, kind="ExternalInput")
    d_dw = nc.dram_tensor("dw", [4, 16, 128, 2048], F32, kind="ExternalInput")
    d_nw = nc.dram_tensor("nw", [128, 64], F32, kind="ExternalInput")
    d_posb = nc.dram_tensor("posb", [128, S_KV], F32, kind="ExternalInput")
    d_ksin = nc.dram_tensor("ksin", [128, S_KV], F32, kind="ExternalInput")
    d_kcos = nc.dram_tensor("kcos", [128, S_KV], F32, kind="ExternalInput")
    d_maskL = nc.dram_tensor("maskL", [128, 8 * 9 * 128], BF16, kind="ExternalInput")
    d_ident = nc.dram_tensor("identb", [128, 128], BF16, kind="ExternalInput")
    d_identf = nc.dram_tensor("identf", [128, 128], F32, kind="ExternalInput")
    d_h128 = nc.dram_tensor("h128", [128, 128], BF16, kind="ExternalInput")
    d_h128f = nc.dram_tensor("h128f", [128, 128], F32, kind="ExternalInput")
    d_misc = nc.dram_tensor("miscc", [128, 2], F32, kind="ExternalInput")
    d_out = nc.dram_tensor("out", [S_OWN, H], F32, kind="ExternalOutput")

    _spk = "ExternalOutput" if DEBUG_SPILLS else "Internal"
    d_qf = nc.dram_tensor("qf_sp", [16, 128, S_OWN], BF16, kind=_spk)
    d_kf = nc.dram_tensor("kf_sp", [16, 128, S_KV], BF16, kind=_spk)
    d_vsp = nc.dram_tensor("v_sp", [4, 16, 128, 512], BF16, kind=_spk)
    d_x1 = nc.dram_tensor("x1_sp", [S_OWN, H], F32, kind=_spk)
    d_int = nc.dram_tensor("int_sp", [64, 128, S_OWN], F32, kind=_spk)
    d_dwq = nc.dram_tensor("dwq_sp", [64, 128, 2048], BF16, kind="Internal")
    d_y = nc.dram_tensor("y_sp", [S_OWN, H], F32, kind=_spk)
    d_att = nc.dram_tensor("att_dbg", [16, 128, S_OWN], BF16, kind=_spk)
    d_kro = nc.dram_tensor("kro_dbg", [2, 128, S_KV], BF16, kind=_spk)
    d_qro = nc.dram_tensor("qro_dbg", [2, 128, S_OWN], BF16, kind=_spk)
    d_pt = nc.dram_tensor("pt_dbg", [2, 128, 9 * 128], BF16, kind=_spk)
    d_zq = nc.dram_tensor("zq_dbg", [64, 128, S_OWN], BF16, kind=_spk)
    d_st = nc.dram_tensor("st_dbg", [128, 704], F32, kind=_spk)

    SB_BASE = 17408
    def sb(name, shape, dtype, off):
        per = int(np.prod(shape[1:])) * mybir.dt.size(dtype)
        assert off + per <= 211900, (name, off, per)
        return nc.alloc_sbuf_tensor_at(name, shape, dtype, offset=SB_BASE + off)

    ones_b = sb("ones_b", [128, 1], BF16, 0)
    ones_rb = sb("ones_rb", [1, 128], BF16, 32)
    ones_f = sb("ones_f", [128, 1], F32, 320)
    ones_rf = sb("ones_rf", [1, 128], F32, 352)
    ones_h = sb("ones_h", [128, 1], F16, 896)
    identb = sb("identb_s", [128, 128], BF16, 928)
    identf = sb("identf_s", [128, 128], F32, 1184)
    h128c = sb("h128_s", [128, 128], BF16, 1696)
    h128f = sb("h128f_s", [128, 128], F32, 15360)
    nw = sb("nw_s", [128, 64], F32, 1952)
    misc = sb("misc_s", [128, 2], F32, 2208)
    cst = sb("cst", [128, 16], F32, 2240)
    vrow = sb("vrow", [1, 2048], F32, 2304)
    vrow2 = sb("vrow2", [1, 2048], F32, 10496)
    stA = sb("stA", [128, 64], F32, 18688)
    stB = sb("stB", [128, 64], F32, 18944)
    aws = sb("aws", [128, 12], F32, 19200)
    cosf = sb("cosf", [128, S_KV], F32, 20000)
    sinf = sb("sinf", [128, S_KV], F32, 28192)
    cosrs_k = sb("cosrs_k", [128, S_KV], BF16, 36384)
    sinrs_k = sb("sinrs_k", [128, S_KV], BF16, 40480)
    cosrs_q = sb("cosrs_q", [128, S_OWN], BF16, 44576)
    sinrs_q = sb("sinrs_q", [128, S_OWN], BF16, 46624)
    rsbk = sb("rsbk", [128, S_KV], F32, 48672)
    rsbq = sb("rsbq", [128, S_OWN], F32, 56864)

    A = 80000

    stg = sb("stg", [128, 5120], F32, A)

    kevb = [sb(f"kevb{i}", [128, S_KV], BF16, A + i * 4096) for i in range(2)]
    qevb = [sb(f"qevb{i}", [128, 512], BF16, A + 8192 + i * 1024) for i in range(4)]
    q2f = [sb(f"q2f{i}", [128, 512], F16, A + 12288 + i * 1024) for i in range(2)]
    qkwb = [sb(f"qkwb{i}", [128, 2048], BF16, A + 16384 + i * 4096) for i in range(2)]
    vwb = [sb(f"vwb{i}", [128, 8192], BF16, A + 24576 + i * 16384) for i in range(2)]
    xTt = [sb(f"xTt{k}", [128, S_KV], BF16, A + 57344 + k * 4096) for k in range(16)]

    wredb3 = [sb(f"wredb3{i}", [128, 1024], F32, A + 98304 + i * 4096) for i in range(8)]

    kraw = [sb(f"kraw{i}", [128, S_KV], BF16, A + i * 4096) for i in range(2)]
    qraw = [sb(f"qraw{i}", [128, S_OWN], BF16, A + 8192 + i * 2048) for i in range(2)]
    vsb = [sb(f"vsb{i}", [128, S_KV], BF16, A + 12288 + i * 4096) for i in range(2)]
    kswap = sb("kswap", [128, S_KV], BF16, A + 20480)
    qswap = sb("qswap", [128, S_OWN], BF16, A + 24576)
    m1k = sb("m1k", [128, S_KV], BF16, A + 26624)
    m2k = sb("m2k", [128, S_KV], BF16, A + 30720)
    krot = [sb(f"krot{i}", [128, S_KV], BF16, A + 34816 + i * 4096) for i in range(2)]
    m1q = sb("m1q", [128, S_OWN], BF16, A + 43008)
    m2q = sb("m2q", [128, S_OWN], BF16, A + 45056)
    qrot = [sb(f"qrot{i}", [128, S_OWN], BF16, A + 47104 + i * 2048) for i in range(2)]
    ptil = [sb(f"ptil{i}", [128, 9 * 128], BF16, A + 51200 + i * 2304) for i in range(2)]
    ptT = [sb(f"ptT{i}", [128, 9 * 128], BF16, A + 55808 + i * 2304) for i in range(2)]
    nms = sb("nms", [128, 32], F32, A + 60416)
    wcols = sb("wcols", [128, 384], F32, A + 60544)
    maskL = sb("maskLs", [128, 8 * 9 * 128], BF16, 61184)
    attn = [sb(f"attn{h}", [128, S_OWN], BF16, A + 65536 + h * 2048) for h in range(16)]

    y1t = [sb(f"y1t{i}", [128, H], F32, A + i * 8192) for i in range(8)]
    wowb = [sb(f"wowb{i}", [128, 8192], BF16, 20000 + i * 16384) for i in range(2)]
    xownb = [sb(f"xownb{i}", [128, H], F32, 52768 + i * 8192) for i in range(2)]
    sqscr = sb("sqscr", [128, H], F32, 69152)
    sqscr2 = sb("sqscr2", [128, H], F32, 20000)
    xqs = [sb(f"xqs{i}", [128, H], BF16, 28192 + i * 4096) for i in range(2)]
    arow = sb("arow", [1, 1024], F32, 36384)
    xqT = [sb(f"xqT{k}", [128, S_OWN], BF16, A + 65536 + k * 2048) for k in range(16)]
    cgb = sb("cgb", [128, S_OWN], F32, A + 98304)
    cub = sb("cub", [128, S_OWN], F32, A + 102400)
    amcb = sb("amcb", [128, S_OWN], F32, A + 106496)

    gwf = [sb(f"gwf{i}", [128, 2048], F32, 20000 + i * 8192) for i in range(2)]
    uwf = [sb(f"uwf{i}", [128, 2048], F32, 36384 + i * 8192) for i in range(2)]
    tsgg1 = [sb(f"tsgg1_{i}", [128, 2048], BF16, 52768 + i * 4096) for i in range(2)]
    tsgg2 = [sb(f"tsgg2_{i}", [128, 2048], BF16, A + 20480 + i * 4096) for i in range(2)]
    tsgu1 = [sb(f"tsgu1_{i}", [128, 2048], BF16, A + 28672 + i * 4096) for i in range(2)]
    tsgu2 = [sb(f"tsgu2_{i}", [128, 2048], BF16, A + 36864 + i * 4096) for i in range(2)]
    wqg = [sb(f"wqg{i}", [128, 2048], BF16, 60960 + i * 4096) for i in range(2)]
    wqu = [sb(f"wqu{i}", [128, 2048], BF16, 69152 + i * 4096) for i in range(2)]
    tgv = sb("tgv", [128, S_OWN], F32, A)
    tgs = sb("tgs", [128, S_OWN], F32, A + 4096)
    tuv = sb("tuv", [128, S_OWN], F32, A + 8192)
    intev = [sb(f"intev{i}", [128, S_OWN], F32, A + 12288 + i * 4096) for i in range(2)]

    zw = [sb(f"zw{k}", [128, 512], F32, 20000 + k * 2048) for k in range(64)]
    Q6 = 152000
    bscr = [sb(f"bscr{i}", [128, 512], F32, Q6 + i * 2048) for i in range(4)]
    mxz = sb("mxz", [128, 512], F32, Q6 + 8192)
    mnz = sb("mnz", [128, 512], F32, Q6 + 10240)
    absz = sb("absz", [128, 512], F32, Q6 + 12288)
    sbcz = sb("sbcz", [128, 512], F32, Q6 + 14336)
    zsc = [sb(f"zsc{i}", [128, 512], F32, Q6 + 16384 + i * 2048) for i in range(4)]
    zsc2 = [sb(f"zsc2_{i}", [128, 512], F32, Q6 + 24576 + i * 2048) for i in range(4)]
    zqe = [sb(f"zqe{i}", [128, 512], BF16, Q6 + 32768 + i * 1024) for i in range(4)]
    arow2 = sb("arow2", [1, 1024], F32, Q6 + 36864)

    Q7 = 152000
    zq7 = [sb(f"zq7_{k}", [128, 512], BF16, 20000 + k * 1024) for k in range(64)]
    dwf = [sb(f"dwf{i}", [128, 2048], F32, Q7 + i * 8192) for i in range(2)]
    td1 = sb("td1", [128, 2048], BF16, Q7 + 16384)
    td2 = sb("td2", [128, 2048], BF16, Q7 + 20480)
    dwq = [sb(f"dwq{i}", [128, 2048], BF16, Q7 + 24576 + i * 4096) for i in range(2)]
    yev = [sb(f"yev{i}", [128, 512], F32, Q7 + 32768 + i * 2048) for i in range(2)]
    x1rd = [sb(f"x1rd{i}", [128, 512], F32, Q7 + 36864 + i * 2048) for i in range(2)]
    sqs7 = sb("sqs7", [128, 512], F32, Q7 + 40960)
    yfin = [sb(f"yfin{i}", [128, H], F32, Q7 + i * 8192) for i in range(2)]
    ofin = [sb(f"ofin{i}", [128, H], F32, Q7 + 16384 + i * 8192) for i in range(2)]

    ps3 = [nc.alloc_psum_tensor(f"ps3_{i}", [128, 1536], F32) for i in range(2)]
    pp = [ps3[i // 3][:, (i % 3) * 512:(i % 3 + 1) * 512] for i in range(6)]
    pbf = [nc.alloc_psum_tensor(f"pbf{i}", [128, 1024], BF16) for i in range(2)]

    ld = Sem(nc, "ld")
    ldy = Sem(nc, "ldy")
    st = Sem(nc, "st")
    t1 = Sem(nc, "t1")
    t2 = Sem(nc, "t2")
    v1 = Sem(nc, "v1")
    v2 = Sem(nc, "v2")
    s1 = Sem(nc, "s1")
    s2 = Sem(nc, "s2")
    g1 = Sem(nc, "g1")
    ALLSEMS = [ld, ldy, st, t1, t2, v1, v2, s1, s2, g1]

    b = B(nc)
    b.dma("g", identb[:], d_ident[:], ld)
    b.dma("g", identf[:], d_identf[:], ld)
    b.dma("g", h128c[:], d_h128[:], ld)
    b.dma("g", h128f[:], d_h128f[:], ld)
    b.dma("g", nw[:], d_nw[:], ld)
    b.dma("g", misc[:], d_misc[:], ld)

    def _ms(e):
        e.memset(ones_b[:], 1.0)
        e.memset(ones_rb[:], 1.0)
        e.memset(ones_f[:], 1.0)
        e.memset(ones_rf[:], 1.0)
        e.memset(ones_h[:], 1.0)
        e.memset(cst[:, 0:1], C52)
        e.memset(cst[:, 1:2], 1.0)
        e.memset(cst[:, 6:7], -0.5)
        e.memset(cst[:, 8:9], -1.0)
        e.memset(cst[:, 9:10], 10000.0)
        return e.memset(cst[:, 14:15], 0.5)
    b.add("g", _ms, inc=g1)

    b.w("v", g1)
    b.add("v", lambda e: e.tensor_scalar(cst[:, 2:3], cst[:, 1:2], cst[:, 0:1], None, op0=OP.add), inc=v1)
    b.w("v", v1)
    b.add("v", lambda e: e.tensor_scalar(cst[:, 3:4], cst[:, 2:3], cst[:, 0:1], None, op0=OP.subtract), inc=v1)
    b.w("v", v1)
    b.add("v", lambda e: e.tensor_scalar(cst[:, 15:16], cst[:, 3:4], C52, None, op0=OP.mult), inc=v1)
    b.w("v", v1)
    b.add("v", lambda e: e.tensor_scalar(cst[:, 4:5], cst[:, 15:16], C23, None, op0=OP.add), inc=v1)
    b.w("v", v1)
    b.add("v", lambda e: e.tensor_scalar(cst[:, 5:6], cst[:, 4:5], -1.0, None, op0=OP.mult), inc=v1)
    b.w("v", v1)
    b.add("v", lambda e: e.tensor_scalar(cst[:, 3:4], cst[:, 4:5], 127.0, None, op0=OP.add), inc=v1)
    b.add("v", lambda e: e.tensor_scalar(cst[:, 2:3], cst[:, 1:2], 127.0, None, op0=OP.mult), inc=v1)

    b.w("s", g1)
    b.add("s", lambda e: e.activation(cst[:, 10:11], cst[:, 9:10], AF.Ln), inc=s1)
    b.w("v", s1)
    b.w("v", ld)
    b.add("v", lambda e: e.tensor_scalar(cst[:, 15:16], misc[:, 0:1], cst[:, 10:11], None, op0=OP.mult), inc=v1)
    b.w("s", v1)
    b.add("s", lambda e: e.activation(cst[:, 12:13], cst[:, 15:16], AF.Exp, scale=cst[:, 8:9]), inc=s1)
    b.w("v", s1)
    b.add("v", lambda e: e.tensor_scalar(cst[:, 13:14], cst[:, 12:13], misc[:, 1:2], None, op0=OP.mult), inc=v1)
    INVFS = v1.n

    for half in range(2):
        lo = half * 1024
        b.w("y", s1, s1.n)
        b.dma("y", stg[:, 0:1024], d_posb[:, lo:lo + 1024], ldy)
        b.dma("y", stg[:, 1024:2048], d_ksin[:, lo:lo + 1024], ldy)
        b.dma("y", stg[:, 2048:3072], d_kcos[:, lo:lo + 1024], ldy)
        b.w("v", ldy)
        b.w("v", v1, INVFS)
        b.add("v", lambda e: e.tensor_scalar(stg[:, 3072:4096], stg[:, 0:1024], cst[:, 13:14], None, op0=OP.mult), inc=v1)
        b.w("v", v1)
        b.add("v", lambda e: e.tensor_copy(stg[:, 4096:5120], stg[:, 3072:4096]), inc=v1)
        b.w("v", v1)
        b.add("v", lambda e: e.scalar_tensor_tensor(stg[:, 3072:4096], stg[:, 1024:2048], -CW1, stg[:, 3072:4096], op0=OP.mult, op1=OP.add))
        b.add("v", lambda e: e.scalar_tensor_tensor(stg[:, 3072:4096], stg[:, 1024:2048], -CW2, stg[:, 3072:4096], op0=OP.mult, op1=OP.add))
        b.add("v", lambda e: e.scalar_tensor_tensor(stg[:, 3072:4096], stg[:, 1024:2048], -CW3, stg[:, 3072:4096], op0=OP.mult, op1=OP.add))
        b.add("v", lambda e: e.scalar_tensor_tensor(stg[:, 4096:5120], stg[:, 2048:3072], -CW1, stg[:, 4096:5120], op0=OP.mult, op1=OP.add))
        b.add("v", lambda e: e.scalar_tensor_tensor(stg[:, 4096:5120], stg[:, 2048:3072], -CW2, stg[:, 4096:5120], op0=OP.mult, op1=OP.add))
        b.add("v", lambda e: e.scalar_tensor_tensor(stg[:, 4096:5120], stg[:, 2048:3072], -CW3, stg[:, 4096:5120], op0=OP.mult, op1=OP.add), inc=v1)
        b.w("s", v1)
        b.add("s", lambda e, _l=lo: e.activation(sinf[:, _l:_l + 1024], stg[:, 3072:4096], AF.Sin), inc=s1)
        b.add("s", lambda e, _l=lo: e.activation(cosf[:, _l:_l + 1024], stg[:, 4096:5120], AF.Sin), inc=s1)
    b.play("p0")

    b = B(nc)
    b.fence(ALLSEMS)
    for k in range(16):
        b.dma("g", xTt[k][:], d_xT[k * 128:(k + 1) * 128, :], ld)
    XT_LD = ld.n

    slab_t = {}
    evict_s = {}
    spill_y = {}

    for fc in range(4):
        buf = vwb[fc % 2]
        if fc >= 2:
            b.w("g", t1, slab_t[("v", fc - 2)])
        b.dma("g", buf[:], d_vw[fc], ld)
        ldn = ld.n
        b.w("t", ld, ldn)
        if fc == 0:
            b.w("t", ld, XT_LD)
        for tokt in range(16):
            ppb = pp[tokt % 2]
            keyp = ("v", fc, tokt - 2) if tokt >= 2 else ("v", fc - 1, 14 + tokt)
            if keyp in evict_s:
                b.w("t", s1, evict_s[keyp])

            def mk(_b=buf, _p=ppb, _tt=tokt):
                def fn(e):
                    ins = None
                    for kt in range(16):
                        ins = e.matmul(_p[:], xTt[kt][:, _tt * 128:(_tt + 1) * 128],
                                       _b[:, kt * 512:(kt + 1) * 512],
                                       start=(kt == 0), stop=(kt == 15))
                    return ins
                return fn
            b.add("t", mk(), inc=t1)
            b.w("s", t1)
            ev = qevb[tokt % 4]
            kspill = ("vs", fc, tokt - 4) if tokt >= 4 else ("vs", fc - 1, 12 + tokt)
            if kspill in spill_y:
                b.w("s", st, spill_y[kspill])
            b.add("s", lambda e, _p=ppb, _e=ev: e.copy(_e[:], _p[:]), inc=s1)
            evict_s[("v", fc, tokt)] = s1.n
            b.w("y", s1)
            b.dma("y", d_vsp[fc, tokt], ev[:], st)
            spill_y[("vs", fc, tokt)] = st.n
        slab_t[("v", fc)] = t1.n

    for sec in ("k", "q"):
        ftlist = list(range(16, 32)) if sec == "k" else list(range(0, 16))
        chunks = [0, 1, 2, 3] if sec == "k" else [2, 3]
        varps = [pp[2], pp[3], pp[4], pp[5]] if sec == "k" else [pp[2], pp[3]]
        dspill = d_kf if sec == "k" else d_qf
        if sec == "q":
            b.w("s", st, st.n)
            b.w("g", t1, t1.n)
        chev = {}
        for i, ft in enumerate(ftlist):
            buf = qkwb[i % 2]
            if i >= 2:
                b.w("g", t1, slab_t[(sec, i - 2)])
            b.dma("g", buf[:], d_qkw[ft], ld)
            ldn = ld.n
            kev = kevb[i % 2]
            if i >= 2:
                b.w("s", st, spill_y[(sec, i - 2)])
            for ci, ch in enumerate(chunks):
                gidx = i * len(chunks) + ci
                ppb = pp[gidx % 2]
                b.w("t", ld, ldn)
                if gidx >= 2:
                    b.w("t", s1, chev[gidx - 2])
                if sec == "k" and i == 0 and ci == 0:
                    b.w("t", ld, XT_LD)

                def mk(_b=buf, _p=ppb, _ch=ch):
                    def fn(e):
                        ins = None
                        for kt in range(16):
                            ins = e.matmul(_p[:], _b[:, kt * 128:(kt + 1) * 128],
                                           xTt[kt][:, _ch * 512:(_ch + 1) * 512],
                                           start=(kt == 0), stop=(kt == 15))
                        return ins
                    return fn
                b.add("t", mk(), inc=t1)
                b.w("s", t1)
                b.add("s", lambda e, _p=ppb, _e=kev, _c=ci: e.copy(_e[:, _c * 512:(_c + 1) * 512], _p[:]), inc=s1)
                q2 = q2f[(i * 4 + ci) % 2]
                b.add("s", lambda e, _p=ppb, _q=q2: e.activation(_q[:], _p[:], AF.Square), inc=s1)
                sqn = s1.n
                chev[gidx] = s1.n
                vp = varps[ci]
                b.w("t", s1, sqn)
                b.add("t", lambda e, _q=q2, _vp=vp, _st=(i == 0), _sp=(i == 15): e.matmul(
                    _vp[0:1, :], ones_h[:], _q[:], start=_st, stop=_sp), inc=t1)
            evict_s[(sec, i)] = s1.n
            b.w("y", s1)
            ncols = 2048 if sec == "k" else 1024
            b.dma("y", dspill[i], kev[:, 0:ncols], st)
            spill_y[(sec, i)] = st.n
            slab_t[(sec, i)] = t1.n
        b.w("s", t1)
        dstrow = vrow if sec == "k" else vrow2
        for ci in range(len(chunks)):
            b.add("s", lambda e, _c=ci, _d=dstrow, _vp=varps[ci]: e.copy(
                _d[0:1, _c * 512:(_c + 1) * 512], _vp[0:1, :]), inc=s1)

    b.add("v", lambda e: e.memset(cst[:, 7:8], 1e-6))
    b.add("v", lambda e: e.memset(cst[:, 15:16], float(1.0 / 2048.0)))
    b.add("v", lambda e: e.memset(cst[:, 11:12], float(128.0 * 1e-6)))
    b.add("v", lambda e: e.memset(cst[:, 12:13], float(1.0 / 16.0)), inc=v1)
    b.w("s", v1)
    b.w("s", s1, s1.n)
    b.add("s", lambda e: e.activation(vrow[0:1, :], vrow[0:1, :], AF.Sqrt,
                                      bias=cst[0:1, 7:8], scale=cst[0:1, 15:16]), inc=s1)
    b.add("s", lambda e: e.activation(vrow2[0:1, 0:1024], vrow2[0:1, 0:1024], AF.Sqrt,
                                      bias=cst[0:1, 11:12], scale=cst[0:1, 12:13]), inc=s1)
    b.w("v", s1)
    b.add("v", lambda e: e.reciprocal(vrow[0:1, :], vrow[0:1, :]), inc=v1)
    b.add("v", lambda e: e.reciprocal(vrow2[0:1, 0:1024], vrow2[0:1, 0:1024]), inc=v1)
    b.w("t", v1, v1.n)
    for ci in range(4):
        b.add("t", lambda e, _c=ci: e.matmul(pp[0][:, :], ones_rf[:], vrow[0:1, _c * 512:(_c + 1) * 512],
                                             start=True, stop=True), inc=t1)
        b.w("s", t1)
        b.add("s", lambda e, _c=ci: e.copy(rsbk[:, _c * 512:(_c + 1) * 512], pp[0][:, :]), inc=s1)
        b.w("t", s1)
    for ci in range(2):
        b.add("t", lambda e, _c=ci: e.matmul(pp[1][:, :], ones_rf[:], vrow2[0:1, _c * 512:(_c + 1) * 512],
                                             start=True, stop=True), inc=t1)
        b.w("s", t1)
        b.add("s", lambda e, _c=ci: e.copy(rsbq[:, _c * 512:(_c + 1) * 512], pp[1][:, :]), inc=s1)
        b.w("t", s1)
    b.w("v", s1)
    b.add("v", lambda e: e.tensor_mul(cosrs_k[:], cosf[:], rsbk[:]), inc=v1)
    b.add("v", lambda e: e.tensor_mul(sinrs_k[:], sinf[:], rsbk[:]), inc=v1)
    b.add("v", lambda e: e.tensor_mul(cosrs_q[:], cosf[:, 1024:2048], rsbq[:]), inc=v1)
    b.add("v", lambda e: e.tensor_mul(sinrs_q[:], sinf[:, 1024:2048], rsbq[:]), inc=v1)
    b.play("p2")

    b = B(nc)
    b.fence(ALLSEMS)
    b.dma("g", maskL[:], d_maskL[:], ld)
    MASK_LD = ld.n

    amsrc = ([d_gw[ft][:, hf * 1024:(hf + 1) * 1024] for ft in range(64) for hf in range(2)]
             + [d_uw[ft][:, hf * 1024:(hf + 1) * 1024] for ft in range(64) for hf in range(2)]
             + [d_dw[fo, kc][:, hf * 1024:(hf + 1) * 1024]
                for fo in range(4) for kc in range(16) for hf in range(2)])
    am_cons = {}
    am_ldn = {}

    def am_issue(c):
        buf = wredb3[c % 8]
        if c - 8 in am_cons:
            b.w("g", *am_cons[c - 8])
        b.dma("g", buf[:], amsrc[c], ld)
        am_ldn[c] = ld.n

    def am_consume(c):
        buf = wredb3[c % 8]
        if c % 3 != 2:
            b.w("s", ld, am_ldn[c])
            b.add("s", lambda e, _b2=buf, _co=c: e.activation(
                _b2[:], _b2[:], AF.Abs, accum_out=wcols[:, _co:_co + 1]), inc=s2)
            am_cons[c] = (s2, s2.n)
        else:
            b.w("v", ld, am_ldn[c])
            b.add("v", lambda e, _b2=buf, _co=c: e.tensor_reduce(
                wcols[:, _co:_co + 1], _b2[:], axis=AX.X, op=OP.add,
                apply_absolute_value=True), inc=v2)
            am_cons[c] = (v2, v2.n)

    rope_v = {}
    SCm = {}
    EXm = {}
    T2m = {}
    E1m = {}
    E2m = {}
    PVm = {}
    AEm = {}
    SLm = {}

    for h in range(16):
        parh = h % 2
        if (h - 2, 7) in PVm:
            b.w("y", t1, PVm[(h - 2, 7)])
        b.dma("y", kraw[parh][:], d_kf[h], ldy)
        b.dma("y", qraw[parh][:], d_qf[h], ldy)
        vsrc = d_vsp[h // 4, :, :, (h % 4) * 128:(h % 4 + 1) * 128].transpose([1, 0, 2])
        b.dma("y", vsb[parh][:], vsrc, ldy)
        KQV_LD = ldy.n
        b.w("y", ldy, KQV_LD)
        if h >= 1:
            b.w("y", v2, rope_v[h - 1])
        b.dma("y", kswap[0:64, :], kraw[parh][64:128, :], ldy)
        b.dma("y", kswap[64:128, :], kraw[parh][0:64, :], ldy)
        b.dma("y", qswap[0:64, :], qraw[parh][64:128, :], ldy)
        b.dma("y", qswap[64:128, :], qraw[parh][0:64, :], ldy)
        SW_LD = ldy.n

        b.w("v", ldy, SW_LD)
        if (h - 2, 7) in SCm:
            b.w("v", t1, SCm[(h - 2, 7)])
        b.add("v", lambda e, _p=parh, _h=h: e.scalar_tensor_tensor(m1k[:], kraw[_p][:], nw[:, 16 + _h:17 + _h], cosrs_k[:], op0=OP.mult, op1=OP.mult), inc=v2)
        b.add("v", lambda e, _h=h: e.scalar_tensor_tensor(m2k[:], kswap[:], nw[:, 48 + _h:49 + _h], sinrs_k[:], op0=OP.mult, op1=OP.mult), inc=v2)
        b.w("v", v2)
        b.add("v", lambda e, _p=parh: e.tensor_add(krot[_p][:], m1k[:], m2k[:]), inc=v2)
        b.add("v", lambda e, _p=parh, _h=h: e.scalar_tensor_tensor(m1q[:], qraw[_p][:], nw[:, 0 + _h:1 + _h], cosrs_q[:], op0=OP.mult, op1=OP.mult), inc=v2)
        b.add("v", lambda e, _h=h: e.scalar_tensor_tensor(m2q[:], qswap[:], nw[:, 32 + _h:33 + _h], sinrs_q[:], op0=OP.mult, op1=OP.mult), inc=v2)
        b.w("v", v2)
        b.add("v", lambda e, _p=parh: e.tensor_add(qrot[_p][:], m1q[:], m2q[:]), inc=v2)
        rope_v[h] = v2.n
        RON = v2.n
        if DEBUG_SPILLS and h in (8, 9):
            b.w("y", v2, RON)
            b.dma("y", d_kro[h - 8], krot[parh][:], st)
            b.dma("y", d_qro[h - 8], qrot[parh][:], st)

        for qt in range(8):
            par = qt % 2
            u = h * 8 + qt
            base = par * 8
            prev = (h, qt - 2) if qt >= 2 else (h - 1, 6 + qt)

            for c in (range((u - 2) * 3, (u - 2) * 3 + 3) if u >= 2 else []):
                am_consume(c)
            for c in range(u * 3, u * 3 + 3):
                if c < 384:
                    am_issue(c)

            b.w("t", v2, RON)
            if prev in EXm:
                b.w("t", s1, EXm[prev])
            if u == 0:
                b.w("t", ld, MASK_LD)

            def mk_sc(_par=par, _qt=qt, _ph=parh):
                def fn(e):
                    ps = ps3[_par]
                    for sg, (c0, cw) in enumerate(((0, 512), (512, 512), (1024, 128))):
                        e.matmul(ps[:, c0:c0 + cw], qrot[_ph][:, _qt * 128:(_qt + 1) * 128],
                                 krot[_ph][:, _qt * 128 + c0:_qt * 128 + c0 + cw], start=True, stop=False)
                        ins = e.matmul(ps[:, c0:c0 + cw], identb[:],
                                       maskL[:, (_qt * 9) * 128 + c0:(_qt * 9) * 128 + c0 + cw],
                                       start=False, stop=True)
                    return ins
                return fn
            b.add("t", mk_sc(), inc=t1)
            SCm[(h, qt)] = t1.n

            b.w("v", t1, SCm[(h, qt)])
            b.add("v", lambda e, _par=par, _base=base: e.tensor_reduce(
                nms[:, _base + 3:_base + 4], ps3[_par][:, 0:1152], axis=AX.X, op=OP.max, negate=True), inc=v1)
            NMn = v1.n

            b.w("s", v1, NMn)
            if prev in T2m:
                b.w("s", t1, T2m[prev])

            def mk_ex(_par=par, _base=base):
                def fn(e):
                    return e.activation(ptil[_par][:, 0:1152], ps3[_par][:, 0:1152], AF.Exp,
                                        bias=nms[:, _base + 3:_base + 4], accum_out=nms[:, _base + 4:_base + 5])
                return fn
            b.add("s", mk_ex(), inc=s1)
            EXm[(h, qt)] = s1.n

            b.w("v", s1, EXm[(h, qt)])
            b.add("v", lambda e, _base=base: e.reciprocal(nms[:, _base + 7:_base + 8], nms[:, _base + 4:_base + 5]), inc=v1)
            b.w("v", v1)
            b.add("v", lambda e, _par=par, _base=base: e.tensor_scalar(
                ptil[_par][:], ptil[_par][:], nms[:, _base + 7:_base + 8], None, op0=OP.mult), inc=v1)
            SLm[(h, qt)] = v1.n

            b.w("t", v1, SLm[(h, qt)])

            def mk_t1(_par=par):
                def fn(e):
                    ins = None
                    for j in range(8):
                        ins = e.transpose(pbf[_par][:, j * 128:(j + 1) * 128],
                                          ptil[_par][:, j * 128:(j + 1) * 128], identb[:])
                    return ins
                return fn
            b.add("t", mk_t1(), inc=t1)
            T1n = t1.n
            b.w("s", t1, T1n)
            b.add("s", lambda e, _par=par: e.copy(ptT[_par][:, 0:1024], pbf[_par][:, 0:1024]), inc=s1)
            E1m[(h, qt)] = s1.n
            b.w("t", s1, E1m[(h, qt)])
            b.add("t", lambda e, _par=par: e.transpose(pbf[_par][:, 0:128], ptil[_par][:, 1024:1152], identb[:]), inc=t1)
            T2m[(h, qt)] = t1.n
            b.w("s", t1, T2m[(h, qt)])
            b.add("s", lambda e, _par=par: e.copy(ptT[_par][:, 1024:1152], pbf[_par][:, 0:128]), inc=s1)
            E2m[(h, qt)] = s1.n

            b.w("t", s1, E2m[(h, qt)])

            def mk_pv(_par=par, _qt=qt, _ph=parh):
                def fn(e):
                    ins = None
                    for bp in range(9):
                        kt = _qt + bp
                        ins = e.matmul(ps3[_par][:, 1152:1280], vsb[_ph][:, kt * 128:(kt + 1) * 128],
                                       ptT[_par][:, bp * 128:(bp + 1) * 128],
                                       start=(bp == 0), stop=(bp == 8))
                    return ins
                return fn
            b.add("t", mk_pv(), inc=t1)
            PVm[(h, qt)] = t1.n

            b.w("s", t1, PVm[(h, qt)])
            b.add("s", lambda e, _par=par, _h=h, _qt=qt: e.copy(
                attn[_h][:, _qt * 128:(_qt + 1) * 128], ps3[_par][:, 1152:1280]), inc=s1)
            AEm[(h, qt)] = s1.n
    for c in range(126 * 3, 384):
        am_consume(c)
    if DEBUG_SPILLS:
        b.w("y", s1, s1.n)
        for hh in range(16):
            b.dma("y", d_att[hh], attn[hh][:], st)
    b.play("p3")

    b = B(nc)
    b.fence(ALLSEMS)
    for wi in range(3):
        b.add("v", lambda e, _w=wi: e.tensor_reduce(
            aws[:, 3 + _w:4 + _w], wcols[:, _w * 128:(_w + 1) * 128], axis=AX.X, op=OP.add), inc=v1)
    b.w("t", v1)
    for wi in range(3):
        b.add("t", lambda e, _c=3 + wi: e.matmul(pp[0][0:1, 0:1], ones_f[:], aws[:, _c:_c + 1],
                                                 start=True, stop=True), inc=t1)
        b.w("s", t1)
        b.add("s", lambda e, _c=8 + wi: e.copy(aws[0:1, _c:_c + 1], pp[0][0:1, 0:1]), inc=s1)
        b.w("t", s1)
    b.add("t", lambda e: e.matmul(pp[0][:, 0:3], ones_rf[:], aws[0:1, 8:11], start=True, stop=True), inc=t1)
    b.w("v", t1)
    b.add("v", lambda e: e.tensor_scalar(aws[:, 3:6], pp[0][:, 0:3], float(1.0 / (8192.0 * 2048.0)), None, op0=OP.mult), inc=v1)
    AMFV = v1.n
    b.w("v", v1)
    b.add("v", lambda e: e.tensor_scalar(aws[:, 3:6], aws[:, 3:6], 1e-5, None, op0=OP.max), inc=v1)
    b.w("v", v1)
    b.add("v", lambda e: e.tensor_scalar(aws[:, 0:3], aws[:, 3:6], 0.5, None, op0=OP.mult), inc=v1)
    b.w("v", v1)
    b.add("v", lambda e: e.reciprocal(aws[:, 3:6], aws[:, 3:6]), inc=v1)

    wo_t = {}
    ev_v = {}
    xo_ld = {}
    for fc in range(4):
        buf = wowb[fc % 2]
        if fc >= 2:
            b.w("g", t1, wo_t[fc - 2])
        b.dma("g", buf[:], d_wow[fc], ld)
        ldn = ld.n
        for tokt in range(8):
            ppb = pp[tokt % 2]
            b.w("t", ld, ldn)
            if fc == 0 and tokt == 0:
                b.w("t", v1, AMFV)
            keyp = (fc, tokt - 2) if tokt >= 2 else (fc - 1, 6 + tokt)
            if keyp in ev_v:
                b.w("t", v1, ev_v[keyp])

            def mk(_b=buf, _p=ppb, _tt=tokt):
                def fn(e):
                    ins = None
                    for hh in range(16):
                        ins = e.matmul(_p[:], attn[hh][:, _tt * 128:(_tt + 1) * 128],
                                       _b[:, hh * 512:(hh + 1) * 512],
                                       start=(hh == 0), stop=(hh == 15))
                    return ins
                return fn
            b.add("t", mk(), inc=t1)
            b.w("v", t1)
            b.add("v", lambda e, _p=ppb, _tt=tokt, _fc=fc: e.tensor_copy(
                y1t[_tt][:, _fc * 512:(_fc + 1) * 512], _p[:]), inc=v1)
            ev_v[(fc, tokt)] = v1.n
        wo_t[fc] = t1.n

    WON = v1.n
    for tokt in range(8):
        if tokt >= 2:
            b.w("y", v1, xo_ld.get(tokt - 2, 0))
        b.dma("y", xownb[tokt % 2][:], d_xown[tokt * 128:(tokt + 1) * 128, :], ldy)
        XOL = ldy.n
        b.w("v", ldy, XOL)
        b.w("v", v1, WON)
        b.add("v", lambda e, _tt=tokt: e.tensor_add(y1t[_tt][:], y1t[_tt][:], xownb[_tt % 2][:]), inc=v1)
        xo_ld[tokt] = v1.n

    b.w("v", v1, v1.n)
    for tokt in range(8):
        b.add("v", lambda e, _t=tokt: e.tensor_reduce(stA[:, _t:_t + 1], y1t[_t][:], axis=AX.X, op=OP.add), inc=v1)
    b.w("s", v1)
    for tokt in range(8):
        b.add("s", lambda e, _t=tokt: e.activation(sqscr[:], y1t[_t][:], AF.Square, accum_out=stA[:, 8 + _t:9 + _t]), inc=s1)
    b.w("v", s1)
    b.add("v", lambda e: e.tensor_scalar(stA[:, 0:8], stA[:, 0:8], float(1.0 / 2048.0), None, op0=OP.mult), inc=v1)
    b.w("v", v1)
    b.add("v", lambda e: e.tensor_scalar(stA[:, 8:16], stA[:, 8:16], float(1.0 / 2048.0), None, op0=OP.mult), inc=v1)
    b.w("v", v1)
    b.add("v", lambda e: e.tensor_mul(stA[:, 56:64], stA[:, 0:8], stA[:, 0:8]), inc=v1)
    b.w("v", v1)
    b.add("v", lambda e: e.tensor_sub(stA[:, 8:16], stA[:, 8:16], stA[:, 56:64]), inc=v1)
    b.add("v", lambda e: e.memset(cst[:, 7:8], 1e-5), inc=v1)
    b.w("s", v1)
    b.add("s", lambda e: e.activation(stA[:, 16:24], stA[:, 8:16], AF.Sqrt, bias=cst[:, 7:8]), inc=s1)
    b.w("v", s1)
    b.add("v", lambda e: e.reciprocal(stA[:, 16:24], stA[:, 16:24]), inc=v1)
    b.w("v", v1)
    b.add("v", lambda e: e.tensor_mul(stA[:, 24:32], stA[:, 0:8], stA[:, 16:24]), inc=v1)
    b.w("v", v1)
    b.add("v", lambda e: e.tensor_scalar(stA[:, 24:32], stA[:, 24:32], -1.0, None, op0=OP.mult), inc=v1)
    LN1N = v1.n
    b.w("s", v1, LN1N)
    for tokt in range(8):
        b.add("s", lambda e, _t=tokt: e.activation(y1t[_t][:], y1t[_t][:], AF.Identity,
                                                   bias=stA[:, 24 + _t:25 + _t], scale=stA[:, 16 + _t:17 + _t]), inc=s1)
    X1N = s1.n
    b.w("y", s1, X1N)
    for tokt in range(8):
        b.dma("y", d_x1[tokt * 128:(tokt + 1) * 128, :], y1t[tokt][:], st)
    b.w("v", s1, X1N)
    for tokt in range(8):
        b.add("v", lambda e, _t=tokt: e.tensor_reduce(stA[:, 32 + _t:33 + _t], y1t[_t][:], axis=AX.X, op=OP.max,
                                                      apply_absolute_value=True), inc=v1)
    b.w("v", v1)
    b.add("v", lambda e: e.tensor_scalar(stA[:, 32:40], stA[:, 32:40], 1e-5, None, op0=OP.max), inc=v1)
    b.w("v", v1)
    b.add("v", lambda e: e.reciprocal(stA[:, 40:48], stA[:, 32:40]), inc=v1)
    b.w("v", v1)
    b.add("v", lambda e: e.tensor_scalar(stA[:, 40:48], stA[:, 40:48], 128.0, None, op0=OP.mult), inc=v1)
    b.w("v", v1)
    b.add("v", lambda e: e.tensor_scalar(stA[:, 48:56], stA[:, 32:40], float(1.0 / 128.0), None, op0=OP.mult), inc=v1)
    QSN = v1.n

    xq_rel = {}
    for tokt in range(8):
        b.w("s", v1, QSN)
        b.add("s", lambda e, _t=tokt: e.activation(sqscr[:], y1t[_t][:], AF.Identity, scale=stA[:, 40 + _t:41 + _t]), inc=s1)
        b.w("v", s1)
        b.add("v", lambda e: e.tensor_scalar(sqscr2[:], sqscr[:], cst[:, 4:5], None, op0=OP.add), inc=v1)
        b.w("v", v1)
        b.add("v", lambda e: e.tensor_scalar(sqscr[:], sqscr2[:], cst[:, 5:6], None, op0=OP.add), inc=v1)
        b.w("v", v1)
        if tokt >= 2:
            b.w("v", t1, xq_rel[tokt - 2])
        b.add("v", lambda e, _t=tokt: e.tensor_scalar(xqs[_t % 2][:], sqscr[:], 127.0, None, op0=OP.min), inc=v1)
        XQV = v1.n
        b.w("t", v1, XQV)
        for k in range(16):
            b.add("t", lambda e, _t=tokt, _k=k: e.transpose(
                pbf[0][:, 0:128], xqs[_t % 2][:, _k * 128:(_k + 1) * 128], identb[:]), inc=t1)
            b.w("s", t1)
            b.add("s", lambda e, _t=tokt, _k=k: e.copy(
                xqT[_k][:, _t * 128:(_t + 1) * 128], pbf[0][:, 0:128]), inc=s1)
            b.w("t", s1)
        xq_rel[tokt] = t1.n

    for tokt in range(8):
        b.add("t", lambda e, _t=tokt: e.transpose(pp[4][0:1, 0:128], stA[:, 48 + _t:49 + _t], identf[:]), inc=t1)
        b.w("s", t1)
        b.add("s", lambda e, _t=tokt: e.copy(arow[0:1, _t * 128:(_t + 1) * 128], pp[4][0:1, 0:128]), inc=s1)
        b.w("t", s1)
    b.w("t", s1)
    for ci in range(2):
        b.add("t", lambda e, _c=ci: e.matmul(pp[4][:, :], ones_rf[:], arow[0:1, _c * 512:(_c + 1) * 512],
                                             start=True, stop=True), inc=t1)
        b.w("s", t1)
        b.add("s", lambda e, _c=ci: e.copy(amcb[:, _c * 512:(_c + 1) * 512], pp[4][:, :]), inc=s1)
        b.w("t", s1)
    b.w("v", s1)
    b.add("v", lambda e: e.tensor_scalar(cgb[:], amcb[:], aws[:, 0:1], None, op0=OP.mult), inc=v1)
    b.add("v", lambda e: e.tensor_scalar(cub[:], amcb[:], aws[:, 1:2], None, op0=OP.mult), inc=v1)
    b.play("p4")

    b = B(nc)
    b.fence(ALLSEMS)
    q_s = {}
    m_t = {}
    ad_v = {}
    sil_s = {}
    ev2 = {}
    sp_y = {}
    bank_rel = [0] * 6
    for ft in range(64):
        p2_ = ft % 2
        if ft >= 2:
            b.w("g", s1, q_s[ft - 2])
        b.dma("g", gwf[p2_][:], d_gw[ft], ld)
        b.dma("g", uwf[p2_][:], d_uw[ft], ld)
        ldn = ld.n
        b.w("s", ld, ldn)
        if ft >= 2:
            b.w("s", v1, ad_v[ft - 2])
        b.add("s", lambda e, _p=p2_: e.activation(tsgg1[_p][:], gwf[_p][:], AF.Sign, bias=cst[:, 6:7], scale=aws[:, 3:4]), inc=s1)
        b.add("s", lambda e, _p=p2_: e.activation(tsgg2[_p][:], gwf[_p][:], AF.Sign, bias=cst[:, 14:15], scale=aws[:, 3:4]), inc=s1)
        b.add("s", lambda e, _p=p2_: e.activation(tsgu1[_p][:], uwf[_p][:], AF.Sign, bias=cst[:, 6:7], scale=aws[:, 4:5]), inc=s1)
        b.add("s", lambda e, _p=p2_: e.activation(tsgu2[_p][:], uwf[_p][:], AF.Sign, bias=cst[:, 14:15], scale=aws[:, 4:5]), inc=s1)
        q_s[ft] = s1.n
        b.w("v", s1, q_s[ft])
        if ft >= 2:
            b.w("v", t1, m_t[ft - 2])
        b.add("v", lambda e, _p=p2_: e.tensor_add(wqg[_p][:], tsgg1[_p][:], tsgg2[_p][:]), inc=v1)
        b.add("v", lambda e, _p=p2_: e.tensor_add(wqu[_p][:], tsgu1[_p][:], tsgu2[_p][:]), inc=v1)
        ad_v[ft] = v1.n

        def post(pf):
            b.w("v", t1, m_t[pf])
            if pf >= 1:
                b.w("v", s1, sil_s[pf - 1])
            for j in range(4):
                bk = (pf * 4 + j) % 6
                ch = j % 2
                dst = tgv if j < 2 else tuv
                csb = cgb if j < 2 else cub
                b.add("v", lambda e, _d=dst, _p=pp[bk], _c=csb, _ch=ch: e.tensor_mul(
                    _d[:, _ch * 512:(_ch + 1) * 512], _p[:], _c[:, _ch * 512:(_ch + 1) * 512]), inc=v1)
                bank_rel[bk] = v1.n
            b.w("s", v1)
            b.add("s", lambda e: e.activation(tgs[:], tgv[:], AF.Silu), inc=s1)
            sil_s[pf] = s1.n
            b.w("v", s1)
            if pf >= 2:
                b.w("v", st, sp_y[pf - 2])
            b.add("v", lambda e, _f=pf: e.tensor_mul(intev[_f % 2][:], tgs[:], tuv[:]), inc=v2)
            ev2[pf] = v2.n
            b.w("y", v2)
            b.dma("y", d_int[pf], intev[pf % 2][:], st)
            sp_y[pf] = st.n

        if ft >= 1:
            post(ft - 1)

        b.w("t", v1, ad_v[ft])
        for j in range(4):
            bk = (ft * 4 + j) % 6
            wq = wqg[p2_] if j < 2 else wqu[p2_]
            ch = j % 2
            if bank_rel[bk]:
                b.w("t", v1, bank_rel[bk])

            def mk(_w=wq, _p=pp[bk], _ch=ch):
                def fn(e):
                    ins = None
                    for kt in range(16):
                        ins = e.matmul(_p[:], _w[:, kt * 128:(kt + 1) * 128],
                                       xqT[kt][:, _ch * 512:(_ch + 1) * 512],
                                       start=(kt == 0), stop=(kt == 15))
                    return ins
                return fn
            b.add("t", mk(), inc=t1)
        m_t[ft] = t1.n
    post(63)
    b.play("p5")

    b = B(nc)
    b.fence(ALLSEMS)
    QMUL = {}
    QS3 = {}
    SPL6 = {}
    STATRD = 0
    SBCZn = 0
    for h2 in range(2):
        lo = h2 * 512
        for k in range(64):
            if h2 == 1:
                lk = log_of_phys.get(id(zw[k]))
                b.w("g", v1, QMUL.get((0, lk), BFN0) if lk is not None else BFN0)
            b.dma("g", zw[k][:], d_int[k, :, lo:lo + 512], ld)
        LDALL = ld.n
        b.w("v", ld, LDALL)
        if h2 == 1:
            b.w("v", st, st.n)
        tl = list(zw)
        sc = list(bscr)
        sched = []
        for stg_i in range(6):
            stride = 1 << stg_i
            si = 0
            for base2 in range(0, 64, stride * 2):
                for off2 in range(stride):
                    a2 = base2 + off2
                    c2 = a2 + stride
                    s_t = sc[si % 4]
                    sched.append((s_t, tl[a2], tl[c2]))
                    sc[si % 4] = tl[c2]
                    tl[c2] = s_t
                    si += 1

        def mk_b(_sched=list(sched)):
            def fn(e):
                ins = None
                for s_t, a_t, c_t in _sched:
                    e.tensor_sub(s_t[:], a_t[:], c_t[:])
                    ins = e.tensor_add(a_t[:], a_t[:], c_t[:])
                return ins
            return fn
        b.add("v", mk_b(), inc=v1)
        BFN = v1.n
        if h2 == 0:
            BFN0 = BFN
        ztl = list(tl)
        if h2 == 0:
            log_of_phys = {id(hd): j for j, hd in enumerate(ztl)}

        b.w("t", v1, BFN)
        if h2 == 1:
            b.w("t", v1, STATRD)
            b.w("t", s1, SBCZn)
        bankev = getattr(b, "_bankev", None) or [0] * 6
        for k in range(64):
            ppb = pp[k % 6]
            if bankev[k % 6]:
                b.w("t", s1, bankev[k % 6])
            b.add("t", lambda e, _z=ztl[k], _p=ppb: e.matmul(_p[:], h128f[:], _z[:], start=True, stop=True), inc=t1)
            b.w("s", t1)
            b.add("s", lambda e, _p=ppb, _z=ztl[k]: e.copy(_z[:], _p[:]), inc=s1)
            bankev[k % 6] = s1.n
            b.w("v", s1)
            if k == 0:
                b.add("v", lambda e, _z=ztl[0]: e.tensor_copy(mxz[:], _z[:]))
                b.add("v", lambda e, _z=ztl[0]: e.tensor_copy(mnz[:], _z[:]), inc=v1)
            else:
                b.add("v", lambda e, _z=ztl[k]: e.tensor_max(mxz[:], mxz[:], _z[:]))
                b.add("v", lambda e, _z=ztl[k]: e.tensor_tensor(mnz[:], mnz[:], _z[:], op=OP.min), inc=v1)
        b._bankev = bankev
        b.w("v", v1)
        b.add("v", lambda e: e.tensor_scalar(absz[:], mnz[:], -1.0, None, op0=OP.mult), inc=v1)
        b.w("v", v1)
        b.add("v", lambda e: e.tensor_max(absz[:], absz[:], mxz[:]), inc=v1)
        ABSN = v1.n
        b.w("t", v1, ABSN)
        for j in range(4):
            b.add("t", lambda e, _j=j: e.transpose(pp[_j][:, 0:128], absz[:, _j * 128:(_j + 1) * 128], identf[:]), inc=t1)
        TRN = t1.n
        b.w("v", t1, TRN)
        for j in range(4):
            b.add("v", lambda e, _j=j, _h=h2: e.tensor_reduce(
                stB[:, 32 + 4 * _h + _j:33 + 4 * _h + _j], pp[_j][:, 0:128], axis=AX.X, op=OP.max), inc=v1)
        STATRD = v1.n
        cA, cB, cC = 32 + 4 * h2, 0 + 4 * h2, 8 + 4 * h2
        b.w("v", v1)
        b.add("v", lambda e, _a=cA, _b2=cB: e.tensor_scalar(stB[:, _b2:_b2 + 4], stB[:, _a:_a + 4], INV_N_FF, None, op0=OP.mult), inc=v1)
        if DEBUG_SPILLS and h2 == 0:
            b.w("y", v1, v1.n)
            b.dma("y", d_st[:, 576:640], stB[:], st)
            b.w("v", st, st.n)
        b.w("v", v1)
        b.add("v", lambda e, _b2=cB: e.tensor_scalar(stB[:, _b2:_b2 + 4], stB[:, _b2:_b2 + 4], 1e-5, None, op0=OP.max), inc=v1)
        b.w("v", v1)
        b.add("v", lambda e, _b2=cB, _c=cC: e.reciprocal(stB[:, _c:_c + 4], stB[:, _b2:_b2 + 4]), inc=v1)
        b.w("v", v1)
        b.add("v", lambda e, _c=cC: e.tensor_scalar(stB[:, _c:_c + 4], stB[:, _c:_c + 4], float(128.0 * np.float64(INV_N_FF)), None, op0=OP.mult), inc=v1)
        b.w("v", v1)
        b.add("v", lambda e, _b2=cB, _o=48 + 4 * h2: e.tensor_scalar(stB[:, _o:_o + 4], stB[:, _b2:_b2 + 4], float(1.0 / 128.0), None, op0=OP.mult), inc=v1)
        b.w("v", v1)
        b.add("v", lambda e, _o=48 + 4 * h2: e.tensor_scalar(stB[:, _o:_o + 4], stB[:, _o:_o + 4], aws[:, 2:3], None, op0=OP.mult), inc=v1)
        SCN = v1.n
        if DEBUG_SPILLS and h2 == 0:
            b.w("y", v1, SCN)
            b.dma("y", d_st[:, 640:704], stB[:], st)
            b.w("v", st, st.n)
            b.w("t", st, st.n)
        b.w("t", v1, SCN)
        for j in range(4):
            b.add("t", lambda e, _j=j, _c=cC: e.transpose(
                pp[4][0:1, _j * 128:(_j + 1) * 128], stB[:, _c + _j:_c + _j + 1], identf[:]), inc=t1)
        b.w("s", t1)
        b.add("s", lambda e: e.copy(arow2[0:1, 0:512], pp[4][0:1, 0:512]), inc=s1)
        b.w("t", s1)
        b.add("t", lambda e: e.matmul(pp[5][:], ones_rf[:], arow2[0:1, 0:512], start=True, stop=True), inc=t1)
        b.w("s", t1)
        b.add("s", lambda e: e.copy(sbcz[:], pp[5][:]), inc=s1)
        SBCZn = s1.n
        if DEBUG_SPILLS and h2 == 0:
            b.w("y", s1, SBCZn)
            b.dma("y", d_st[:, 0:64], stB[:], st)
            b.dma("y", d_st[:, 64:576], sbcz[:], st)

        for k in range(64 if not globals().get("DEBUG_NOQ") else 1):
            c4 = k % 4
            b.w("v", s1, SBCZn if k == 0 else QS3.get((h2, k - 4), SBCZn))
            if (h2, k - 4) in SPL6:
                b.w("v", st, SPL6[(h2, k - 4)])
            b.add("v", lambda e, _z=ztl[k], _c=c4: e.tensor_mul(zsc[_c][:], _z[:], sbcz[:]), inc=v1)
            QMUL[(h2, k)] = v1.n
            b.w("s", v1, QMUL[(h2, k)])
            b.add("s", lambda e, _c=c4: e.activation(zsc2[_c][:], zsc[_c][:], AF.Identity, bias=cst[:, 4:5]), inc=s1)
            b.w("s", s1)
            b.add("s", lambda e, _c=c4: e.activation(zsc[_c][:], zsc2[_c][:], AF.Sign if False else AF.Relu, bias=cst[:, 3:4], scale=cst[:, 8:9]), inc=s1)
            b.w("s", s1)
            b.add("s", lambda e, _c=c4: e.activation(zqe[_c][:], zsc[_c][:], AF.Identity, bias=cst[:, 2:3], scale=cst[:, 8:9]), inc=s1)
            QS3[(h2, k)] = s1.n
            b.w("y", s1, QS3[(h2, k)])
            b.dma("y", d_zq[k][:, lo:lo + 512], zqe[c4][:], st)
            SPL6[(h2, k)] = st.n
    b.play("p6")

    b = B(nc)
    b.fence(ALLSEMS)
    b.add("v", lambda e: e.memset(stB[:, 0:16], 0.0), inc=v1)
    dn_q = {}
    dn_m = {}
    ev7 = {}
    SPL7 = {}
    DWSP = {}
    TD_REL = []
    for grp in range(2):
        if grp == 1:
            b.w("g", t1, dn_m[(0, 3, 15)])
        for k in range(64):
            b.dma("g", zq7[k][:], d_zq[k][:, grp * 512:(grp + 1) * 512], ld)
        ZQLD = ld.n
        for fo in range(4):
            for kc in range(16):
                buf = dwf[kc % 2]
                gkey = (grp, fo, kc)
                if kc >= 2:
                    pq = (grp, fo, kc - 2)
                elif fo >= 1:
                    pq = (grp, fo - 1, 14 + kc)
                else:
                    pq = (grp - 1, 3, 14 + kc)
                if grp == 0:
                    if pq in dn_q:
                        b.w("g", s1, dn_q[pq])
                    b.dma("g", buf[:], d_dw[fo, kc], ld)
                    ldn = ld.n
                    b.w("s", ld, ldn)
                    if pq in dn_m:
                        b.w("s", t1, dn_m[pq])
                    if TD_REL:
                        b.w("s", v1, TD_REL[-1])
                    b.add("s", lambda e, _b=buf: e.activation(td1[:], _b[:], AF.Sign, bias=cst[:, 6:7], scale=aws[:, 5:6]), inc=s1)
                    b.add("s", lambda e, _b=buf: e.activation(td2[:], _b[:], AF.Sign, bias=cst[:, 14:15], scale=aws[:, 5:6]), inc=s1)
                    dn_q[gkey] = s1.n
                    b.w("v", s1)
                    if pq in dn_m:
                        b.w("v", t1, dn_m[pq])
                    if pq in DWSP:
                        b.w("v", st, DWSP[pq])
                    b.add("v", lambda e, _kc=kc: e.tensor_add(dwq[_kc % 2][:], td1[:], td2[:]), inc=v1)
                    TD_REL.append(v1.n)
                    WQN = v1.n
                    b.w("y", v1, WQN)
                    b.dma("y", d_dwq[fo * 16 + kc], dwq[kc % 2][:], st)
                    DWSP[gkey] = st.n
                    b.w("t", v1, WQN)
                else:
                    if pq in dn_m:
                        b.w("g", t1, dn_m[pq])
                    b.dma("g", dwq[kc % 2][:], d_dwq[fo * 16 + kc], ld)
                    ldn = ld.n
                    b.w("t", ld, ldn)
                if fo == 0 and kc == 0:
                    b.w("t", ld, ZQLD)
                if kc == 0:
                    pe = (grp, fo - 1) if fo >= 1 else (grp - 1, 3)
                    if pe in ev7:
                        b.w("t", v1, ev7[pe])

                def mk(_w=dwq[kc % 2], _kc=kc):
                    def fn(e):
                        ins = None
                        for k4 in range(4):
                            kt = _kc * 4 + k4
                            for ti in range(4):
                                ins = e.matmul(pp[ti][:],
                                               zq7[kt][:, ti * 128:(ti + 1) * 128],
                                               _w[:, k4 * 512:(k4 + 1) * 512],
                                               start=(kt == 0), stop=(kt == 63))
                        return ins
                    return fn
                b.add("t", mk(), inc=t1)
                dn_m[gkey] = t1.n
            DMN = t1.n
            for ti in range(4):
                tokt = grp * 4 + ti
                b.dma("y", x1rd[ti % 2][:], d_x1[tokt * 128:(tokt + 1) * 128, fo * 512:(fo + 1) * 512], ldy)
                ldyn = ldy.n
                b.w("v", t1, DMN)
                b.w("v", ldy, ldyn)
                pk = (grp, fo, ti - 2)
                if ti < 2:
                    pk = (grp, fo - 1, ti + 2) if fo >= 1 else (grp - 1, 3, ti + 2)
                if pk in SPL7:
                    b.w("v", st, SPL7[pk])
                b.add("v", lambda e, _ti=ti, _tt=tokt: e.scalar_tensor_tensor(
                    yev[_ti % 2][:], pp[_ti][:], stB[:, 48 + _tt:49 + _tt], x1rd[_ti % 2][:],
                    op0=OP.mult, op1=OP.add), inc=v1)
                YEN = v1.n
                b.w("v", v1, YEN)
                b.add("v", lambda e, _ti=ti: e.tensor_reduce(aws[:, 6:7], yev[_ti % 2][:], axis=AX.X, op=OP.add), inc=v1)
                b.w("v", v1)
                b.add("v", lambda e, _tt=tokt: e.tensor_add(stB[:, _tt:_tt + 1], stB[:, _tt:_tt + 1], aws[:, 6:7]), inc=v1)
                b.w("s", v1, YEN)
                b.add("s", lambda e, _ti=ti: e.activation(sqs7[:], yev[_ti % 2][:], AF.Square, accum_out=aws[:, 7:8]), inc=s1)
                b.w("v", s1)
                b.add("v", lambda e, _tt=tokt: e.tensor_add(stB[:, 8 + _tt:9 + _tt], stB[:, 8 + _tt:9 + _tt], aws[:, 7:8]), inc=v1)
                b.w("y", v1)
                b.dma("y", d_y[tokt * 128:(tokt + 1) * 128, fo * 512:(fo + 1) * 512], yev[ti % 2][:], st)
                SPL7[(grp, fo, ti)] = st.n
            ev7[(grp, fo)] = v1.n
    YALL = st.n

    b.w("v", v1)
    b.add("v", lambda e: e.tensor_scalar(stB[:, 0:8], stB[:, 0:8], float(1.0 / 2048.0), None, op0=OP.mult), inc=v1)
    b.w("v", v1)
    b.add("v", lambda e: e.tensor_scalar(stB[:, 8:16], stB[:, 8:16], float(1.0 / 2048.0), None, op0=OP.mult), inc=v1)
    b.w("v", v1)
    b.add("v", lambda e: e.tensor_mul(stB[:, 56:64], stB[:, 0:8], stB[:, 0:8]), inc=v1)
    b.w("v", v1)
    b.add("v", lambda e: e.tensor_sub(stB[:, 8:16], stB[:, 8:16], stB[:, 56:64]), inc=v1)
    b.w("s", v1)
    b.add("s", lambda e: e.activation(stB[:, 16:24], stB[:, 8:16], AF.Sqrt, bias=cst[:, 7:8]), inc=s1)
    b.w("v", s1)
    b.add("v", lambda e: e.reciprocal(stB[:, 16:24], stB[:, 16:24]), inc=v1)
    b.w("v", v1)
    b.add("v", lambda e: e.tensor_mul(stB[:, 24:32], stB[:, 0:8], stB[:, 16:24]), inc=v1)
    b.w("v", v1)
    b.add("v", lambda e: e.tensor_scalar(stB[:, 24:32], stB[:, 24:32], -1.0, None, op0=OP.mult), inc=v1)
    LN2N = v1.n

    outs = {}
    b.w("y", st, YALL)
    for tokt in range(8):
        b.dma("y", yfin[tokt % 2][:], d_y[tokt * 128:(tokt + 1) * 128, :], ldy)
        ldyn = ldy.n
        b.w("s", ldy, ldyn)
        b.w("s", v1, LN2N)
        if tokt >= 2:
            b.w("s", st, outs[tokt - 2])
        b.add("s", lambda e, _t=tokt: e.activation(ofin[_t % 2][:], yfin[_t % 2][:], AF.Identity,
                                                   bias=stB[:, 24 + _t:25 + _t], scale=stB[:, 16 + _t:17 + _t]), inc=s1)
        b.w("y", s1)
        b.dma("y", d_out[tokt * 128:(tokt + 1) * 128, :], ofin[tokt % 2][:], st)
        outs[tokt] = st.n
    b.w("y", st, st.n)
    b.play("p7")

    return nc


def _host_prep(x, wqkv, qnw, knw, wo, wg, wu, wd):
    f32 = np.float32
    bf = ml_dtypes.bfloat16
    x = np.asarray(x, f32)
    wqkv = np.asarray(wqkv, f32)
    qnw = np.asarray(qnw, f32)
    knw = np.asarray(knw, f32)
    wo = np.asarray(wo, f32)
    wg = np.asarray(wg, f32)
    wu = np.asarray(wu, f32)
    wd = np.asarray(wd, f32)

    qkw = np.ascontiguousarray(wqkv[:4096].reshape(32, 128, 16, 128).transpose(0, 3, 2, 1)).reshape(32, 128, 2048)
    vw = np.ascontiguousarray(wqkv[4096:].reshape(4, 512, 16, 128).transpose(0, 3, 2, 1)).reshape(4, 128, 8192)
    wow = np.ascontiguousarray(wo.reshape(4, 512, 16, 128).transpose(0, 3, 2, 1)).reshape(4, 128, 8192)
    gwa = np.ascontiguousarray(wg.reshape(64, 128, 16, 128).transpose(0, 3, 2, 1)).reshape(64, 128, 2048)
    uwa = np.ascontiguousarray(wu.reshape(64, 128, 16, 128).transpose(0, 3, 2, 1)).reshape(64, 128, 2048)
    dwa = np.ascontiguousarray(wd.reshape(4, 512, 64, 128).transpose(0, 2, 3, 1))
    dwa = np.ascontiguousarray(dwa.reshape(4, 16, 4, 128, 512).transpose(0, 1, 3, 2, 4)).reshape(4, 16, 128, 2048)

    nwarr = np.zeros((128, 64), f32)
    qn2 = qnw.reshape(16, 128)
    kn2 = knw.reshape(16, 128)
    nwarr[:, 0:16] = qn2.T
    nwarr[:, 16:32] = kn2.T
    nwarr[:, 32:48] = np.roll(qn2, -64, axis=1).T
    nwarr[:, 48:64] = np.roll(kn2, -64, axis=1).T

    identbm = np.eye(128, dtype=bf)
    identfm = np.eye(128, dtype=f32)
    iii, jjj = np.meshgrid(np.arange(128), np.arange(128), indexing="ij")
    pc = np.vectorize(lambda a: bin(int(a)).count("1"))(iii & jjj)
    h128m = ((-1.0) ** pc).astype(bf)
    h128fm = ((-1.0) ** pc).astype(f32)
    ii = np.arange(128)
    miscm = np.zeros((128, 2), f32)
    miscm[:, 0] = (ii % 64) / 64.0
    miscm[:, 1] = np.where(ii < 64, -1.0, 1.0)

    invf64 = 10000.0 ** (-(np.arange(64) / 64.0))

    ins = []
    for c in range(8):
        bb, j = c // 4, c % 4
        xo = np.ascontiguousarray(x[bb, j * 1024:(j + 1) * 1024])
        xp = np.ascontiguousarray(x[bb, (j - 1) * 1024:j * 1024]) if j >= 1 else xo
        xTarr = np.ascontiguousarray(np.concatenate([xp, xo], 0).T)

        if j == 0:
            pos = np.concatenate([np.arange(1024), np.arange(1024)]).astype(np.float64)
        else:
            pos = ((j - 1) * 1024 + np.arange(2048)).astype(np.float64)
        sgn = np.where(ii < 64, -1.0, 1.0)
        freq = sgn[:, None] * invf64[ii % 64][:, None] * pos[None, :]
        ksin = np.rint(freq / (2 * np.pi)).astype(f32)
        kcos = (np.rint((freq + np.pi / 2) / (2 * np.pi)) - 0.25).astype(f32)
        posb = np.broadcast_to(pos.astype(f32), (128, 2048)).copy()

        mask = np.zeros((8, 9, 128, 128), np.float32)
        kk = np.arange(128)
        qq = np.arange(128)
        for qt in range(8):
            for bp in range(9):
                jb = qt + bp
                kb = jb * 128 + kk[None, :]
                ql = qt * 128 + qq[:, None]
                valid = (kb >= ql + 1) & (kb <= ql + 1024)
                if j == 0:
                    valid = valid & (kb >= 1024)
                mask[qt, bp] = np.where(valid, 0.0, NEGBIG)
        maskLa = np.ascontiguousarray(mask.transpose(2, 0, 1, 3)).reshape(128, 72 * 128).astype(bf)

        ins.append({
            "xT": xTarr, "xown": xo, "qkw": qkw, "vw": vw, "wow": wow,
            "gw": gwa, "uw": uwa, "dw": dwa, "nw": nwarr, "posb": posb,
            "ksin": ksin, "kcos": kcos, "maskL": maskLa,
            "identb": identbm, "identf": identfm, "h128": h128m, "h128f": h128fm,
            "miscc": miscm,
        })
    return ins


_NC_CACHE = None


def kernel(L0_attn_hidden_states, L0_attn_w_qkv, L0_attn_q_norm_w, L0_attn_k_norm_w,
           L0_attn_w_o, L0_ffn_w_gate, L0_ffn_w_up, L0_ffn_w_down):
    global _NC_CACHE
    ins = _host_prep(L0_attn_hidden_states, L0_attn_w_qkv, L0_attn_q_norm_w,
                     L0_attn_k_norm_w, L0_attn_w_o, L0_ffn_w_gate, L0_ffn_w_up,
                     L0_ffn_w_down)
    if _NC_CACHE is None:
        _NC_CACHE = build_nc()
    res = run_bass_kernel_spmd(_NC_CACHE, ins, core_ids=list(range(8)))
    out = np.zeros((2, 4096, 2048), np.float32)
    for c in range(8):
        bb, j = c // 4, c % 4
        out[bb, j * 1024:(j + 1) * 1024] = res.results[c]["out"]
    return out
